# version 06
# baseline (speedup 1.0000x reference)
import math
import numpy as np
import concourse.bass as bass
import concourse.mybir as mybir
from concourse.bass_utils import run_bass_kernel_spmd

F32 = mybir.dt.float32
BF16 = mybir.dt.bfloat16
AF = mybir.ActivationFunctionType
ALU = mybir.AluOpType
AX = mybir.AxisListType

D, E, DS, TH, ZD, L = 1024, 2048, 128, 64, 4419, 4096
LH, I_FFN = 2048, 2816
NT, NTF = 16, 32
NSH = I_FFN // 8
EPS = 1e-5

_TWO_PI = 2.0 * math.pi
_CW1 = float(np.float32(6.28125))
_CW2 = float(np.float32(_TWO_PI - _CW1))
_CW3 = float(_TWO_PI - _CW1 - float(np.float32(_TWO_PI - _CW1)))
_MAGIC = 12582912.0
_INV2PI = 1.0 / _TWO_PI

C_EPS, C_AB, C_M0, C_M1, C_Q175, C_ZERO, C_ONE, C_HPI = 0, 1, 2, 3, 4, 5, 6, 7
NCST = 8

OFF_DTAL = 0
OFF_SSQ = 96
OFF_MX = 160
OFF_MXF = 176
OFF_ALPHA = 240
OFF_UM = 304
OFF_DTU = 336
OFF_DA = 368
OFF_YTOK = 400
OFF_YOWN = 432
OFF_X1M = 448
OFF_RSTD1 = 464
OFF_TMP = 480
NSME = 512

import concourse.dve_ops as dve_ops
from concourse.dve_op import DveOp
from concourse.dve_spec import Spec, lower, Src0, Src1, C0, C1, C2
from concourse.dve_table_gen import dve_ver_for


def _register_fp4_ops():
    if any(op.name == "ANT_FP4_P1" for op in dve_ops.OPS):
        return
    c1, c2, c3 = (Src0 > C0), (Src0 > C1), (Src0 > C2)
    s_ = (c1 + c2) + (c3 + c3)
    spec1 = Spec(body=s_ + s_)
    spec2 = Spec(body=((Src0 > C0) + (Src0 > C1)) + ((Src0 > C2) + Src1))

    def ref1(in0, in1, s0, s1, imm2):
        a = np.asarray(in0, np.float64)
        return (2.0 * (a > s0) + 2.0 * (a > s1) + 4.0 * (a > imm2)).astype(np.float32)

    def ref2(in0, in1, s0, s1, imm2):
        a = np.asarray(in0, np.float64)
        b = np.asarray(in1, np.float64)
        return ((a > s0) + (a > s1) + (a > imm2) + b).astype(np.float32)

    ver = dve_ver_for("TRN2")
    for name, spec, ref in (("ANT_FP4_P1", spec1, ref1), ("ANT_FP4_P2", spec2, ref2)):
        compiled = lower(spec, ver=ver, perf_en=False, name=name)
        op = DveOp(name, spec, subdim=False,
                   uops_sha={ver: compiled.sha(ver)}, reference=ref)
        dve_ops.OPS.append(op)
        dve_ops.CUSTOM_DVE_SPECS[name] = spec
        dve_ops.CUSTOM_DVE_REFS[name] = ref
        dve_ops._SUB_OPCODE_FOR_NAME[name] = (
            dve_ops._CUSTOM_DVE_ROW_BASE + len(dve_ops.OPS) - 1)


_register_fp4_ops()
FP4_P1 = next(op for op in dve_ops.OPS if op.name == "ANT_FP4_P1")
FP4_P2 = next(op for op in dve_ops.OPS if op.name == "ANT_FP4_P2")


class Sems:
    def __init__(self, nc, es, names):
        self.h = {n: es.enter_context(nc.semaphore(n)) for n in names}
        self.v = {n: 0 for n in names}

    def tick(self, name, n=1):
        self.v[name] += n
        return self.h[name], self.v[name]

    def val(self, name):
        return self.v[name]

    def __getitem__(self, name):
        return self.h[name]


class Arena:
    def __init__(self, nc, es, nbytes):
        self.nwords = nbytes // 4
        self.t = es.enter_context(nc.sbuf_tensor("arena", [128, self.nwords], F32))
        self.free_list = [(0, self.nwords)]
        self.live = {}

    def alloc(self, name, nbytes, dtype=F32, shape=None):
        words = (nbytes + 255) // 256 * 64
        cands = [(sz, idx) for idx, (off, sz) in enumerate(self.free_list)
                 if sz >= words]
        cands.sort()
        for idx in [c[1] for c in cands]:
            off, sz = self.free_list[idx]
            if True:
                if sz == words:
                    self.free_list.pop(idx)
                else:
                    self.free_list[idx] = (off + words, sz - words)
                self.live[name] = (off, words)
                ap = self.t[:, off:off + words]
                if dtype != F32:
                    ap = ap.bitcast(dtype)
                if shape is not None:
                    n = 1
                    for s in shape[1:]:
                        n *= s
                    ap = ap[:, 0:n]
                    if len(shape) == 3:
                        ap = ap.rearrange("p (a b) -> p a b", a=shape[1])
                    elif len(shape) == 4:
                        ap = ap.rearrange("p (a b c) -> p a b c",
                                          a=shape[1], b=shape[2])
                return ap
        raise RuntimeError(f"arena OOM for {name}: need {words}w "
                           f"free={self.free_list} live={list(self.live)}")

    def free(self, name):
        off, words = self.live.pop(name)
        self.free_list.append((off, words))
        self.free_list.sort()
        merged = []
        for off, sz in self.free_list:
            if merged and merged[-1][0] + merged[-1][1] == off:
                merged[-1] = (merged[-1][0], merged[-1][1] + sz)
            else:
                merged.append([off, sz])
        self.free_list = [tuple(x) for x in merged]


def build_nc(debug=(), stage=None):
    nc = bass.Bass("TRN2", target_bir_lowering=False, debug=False,
                   detect_race_conditions=True)
    from contextlib import ExitStack
    es = ExitStack()

    u_in = nc.dram_tensor("u", [L, D], F32, kind="ExternalInput").ap()
    uh_in = nc.dram_tensor("uh", [LH, D], F32, kind="ExternalInput").ap()
    win = nc.dram_tensor("win", [D, ZD], F32, kind="ExternalInput").ap()
    wout = nc.dram_tensor("wout", [E, D], F32, kind="ExternalInput").ap()
    wg_sh = nc.dram_tensor("wg_sh", [NSH, D], F32, kind="ExternalInput").ap()
    wu_sh = nc.dram_tensor("wu_sh", [NSH, D], F32, kind="ExternalInput").ap()
    wd_sh = nc.dram_tensor("wd_sh", [128, I_FFN], F32, kind="ExternalInput").ap()
    vbc_in = nc.dram_tensor("vbc", [128, 512], F32, kind="ExternalInput").ap()
    cst_in = nc.dram_tensor("cst", [128, NCST], F32, kind="ExternalInput").ap()
    id_in = nc.dram_tensor("ident", [128, 128], F32, kind="ExternalInput").ap()
    out_t = nc.dram_tensor("out", [LH, D], F32, kind="ExternalOutput")

    rows_d = nc.dram_tensor("rows_d", [6, L + 64], F32)
    x1_d = nc.dram_tensor("x1_d", [LH, D], F32)
    xq_d = nc.dram_tensor("xq_d", [128, 8, LH], BF16)
    cca_in = nc.dram_tensor("cca_in", [1, 1], F32)
    cca_out = nc.dram_tensor("cca_out", [8, 1], F32)
    ccs_in = nc.dram_tensor("ccs_in", [1, LH], F32)
    ccs_out = nc.dram_tensor("ccs_out", [2, LH], F32)
    WQ_SEC = 8 * 128 * NSH
    ccw_in = nc.dram_tensor("ccw_in", [1, 3 * WQ_SEC], BF16)
    ccw_out = nc.dram_tensor("ccw_out", [8, 3 * WQ_SEC], BF16)

    sem = Sems(nc, es, ["ld", "tr", "cc", "mm", "vec", "act",
                        "gq", "outs", "uld0", "uld1", "uld2", "uld3", "wqld", "wld", "xwld",
                        "gwld", "wobld", "xrl0", "xrl1", "url0", "url1",
                        "tyg", "txq", "thq", "xqs", "xgl", "rbd", "tlu",
                        "stw", "stc", "stm", "stx0", "stx1",
                        "rba", "rbm", "rbw"])

    def sb(name, shape, dt):
        return es.enter_context(nc.sbuf_tensor(name, shape, dt))

    cst = sb("cstt", [128, NCST], F32)
    vbc = sb("vbct", [128, 512], F32)
    ident = sb("identt", [128, 128], F32)
    onesb = sb("onesb", [128, 128], F32)
    asc = sb("asct", [128, 1], F32)
    ascal = sb("ascal", [8, 2], F32)
    smeta = sb("smeta", [128, NSME], F32)
    stat = sb("stat", [128, NTF, 4], F32)
    mx4 = sb("mx4", [128, NT, 4], F32)
    wbc = sb("wbc", [128, 8, 257], BF16)
    wsm = sb("wsm", [128, 8, 67], BF16)
    mxr = sb("mxr", [NT, 128], F32)

    A = Arena(nc, es, 194 * 1024)
    ascr = A.alloc("ascr", 2 * D * 2, BF16, [128, 2 * D])
    lnut = A.alloc("lnut", 4 * D * 2, BF16, [128, 4, D])
    wmv = A.alloc("wmv", 8 * 512 * 2, BF16, [128, 8, 512])
    xsil = A.alloc("xsil", 512 * 2, BF16, [128, 512])
    ps = es.enter_context(nc.psum_tensor("ps", [128, 8, 512], F32))
    blk = es.enter_context(nc.Block())

    Q = {e: [] for e in ("sync", "scalar", "vector", "tensor", "gpsimd")}
    MARKS = {}

    def mark(name):
        MARKS[name] = ({k: len(v) for k, v in Q.items()}, dict(sem.v),
                       len(dbg_list))

    def on(engine):
        def deco(f):
            Q[engine].append(f)
        return deco

    dbg_list = []

    def dbg(name, ap, waits):
        if name not in debug:
            return
        t = nc.dram_tensor("dbg_" + name, list(ap.shape), ap.dtype,
                           kind="ExternalOutput")
        dbg_list.append((t, ap, list(waits)))

    buf_guard = {}
    bank_guard = {}

    def take_buf(name):
        return buf_guard.get(name)

    def guard_buf(name, s, v):
        buf_guard[name] = (s, v)

    def take_bank(i):
        return bank_guard.pop(i, None)

    def guard_bank(i, s, v):
        bank_guard[i] = (s, v)

    def _w(e, g):
        if g is not None:
            e.wait_ge(sem[g[0]], g[1])

    def tr_tick():
        prev = sem.val("tr")
        h, v = sem.tick("tr", 16)
        return h, v, prev

    ZB = lambda p0=0, pn=128: cst[p0:pn, C_ZERO:C_ZERO + 1]

    for dst, src in ((cst[:], cst_in), (vbc[:], vbc_in), (ident[:], id_in)):
        h, _ = sem.tick("ld", 16)

        @on("sync")
        def _(e, d=dst, s=src, h=h):
            e.dma_start(out=d, in_=s).then_inc(h, 16)
    LD_CONST = sem.val("ld")

    h, V_ONES = sem.tick("vec")

    @on("vector")
    def _(e, h=h):
        e.memset(onesb[:], 1.0)
        e.memset(stat[0:1, 3, 0:1], 0.0).then_inc(h, 1)

    wbc_done, wsm_done = [], []
    for d in range(8):
        h1, v1 = sem.tick("wld", 16)
        h2, v2 = sem.tick("wld", 16)
        h3, v3 = sem.tick("wld", 16)
        h4, v4 = sem.tick("wld", 16)
        h5, v5 = sem.tick("wld", 16)

        @on("gpsimd")
        def _(e, d=d, h1=h1, h2=h2, h3=h3, h4=h4):
            r = win[d * 128:(d + 1) * 128, :]
            e.dma_start(out=wbc[:, d, 0:256], in_=r[:, 4096:4352]).then_inc(h1, 16)
            e.dma_start(out=wbc[:, d, 256:257], in_=r[:, 4418:4419],
                        allow_slow_non_contiguous=True).then_inc(h5, 16)
            e.dma_start(out=wsm[:, d, 0:64], in_=r[:, 4354:4418]).then_inc(h2, 16)
            e.dma_start(out=wsm[:, d, 64:66], in_=r[:, 4352:4354]).then_inc(h3, 16)
            e.dma_start(out=wsm[:, d, 66:67], in_=r[:, 4418:4419],
                        allow_slow_non_contiguous=True).then_inc(h4, 16)
        wbc_done.append(v5)
        wsm_done.append(v4)
    WLD_ALL = sem.val("wld")

    lnuH0 = A.alloc("lnuH0", 8 * LH * 2, BF16, [128, 8, LH])
    lnuH1 = A.alloc("lnuH1", 8 * LH * 2, BF16, [128, 8, LH])
    ubuf = A.alloc("ubuf", 4 * D * 4, F32, [128, 4, D])
    N_UB = 4

    def lnu_half(hh):
        return lnuH0 if hh == 0 else lnuH1

    def lnu_mov(d, sl):
        hh, off = sl // 4, (sl % 4) * 512
        return lnu_half(hh)[:, d, off:off + 512]

    def lnu_stat(d, i):
        hh, off = i // 16, (i % 16) * 128
        return lnu_half(hh)[:, d, off:off + 128]

    u_tr = []
    lnt_done = []

    def p1_front(i):
        bslot = i % N_UB
        usem = f"uld{bslot}"
        h_ld, v_ld = sem.tick(usem, 16)

        @on("sync")
        def _(e, i=i, bslot=bslot, h=h_ld, usem=usem):
            if i >= N_UB:
                e.wait_ge(sem["vec"], lnt_done[i - N_UB])
                e.wait_ge(sem[usem], 16 * (i // N_UB))
            else:
                e.wait_ge(sem["ld"], LD_CONST)
            e.dma_start(out=ubuf[:, bslot, :],
                        in_=u_in[i * 128:(i + 1) * 128, :]).then_inc(h, 16)

        h_a1, v_a1 = sem.tick("act")

        @on("scalar")
        def _(e, i=i, bslot=bslot, v_ld=v_ld, usem=usem, h=h_a1, _g0=take_buf("ascr")):
            e.wait_ge(sem[usem], v_ld)
            e.wait_ge(sem["ld"], LD_CONST)
            _w(e, _g0)
            e.activation(out=ascr[:, 0:D], in_=ubuf[:, bslot, :],
                         func=AF.Identity, scale=1.0 / D, bias=ZB(),
                         accum_out=stat[:, i, 0:1])
            e.activation(out=ascr[:, D:2 * D], in_=ubuf[:, bslot, :],
                         func=AF.Square, scale=1.0 / 32.0, bias=ZB(),
                         accum_out=stat[:, i, 1:2]).then_inc(h, 1)
        guard_buf("ascr", "act", v_a1)

        h_v1, v_v1 = sem.tick("vec")

        @on("vector")
        def _(e, i=i, v_a1=v_a1, h=h_v1):
            e.wait_ge(sem["act"], v_a1)
            e.tensor_tensor(out=stat[:, i, 2:3], in0=stat[:, i, 0:1],
                            in1=stat[:, i, 0:1], op=ALU.mult)
            e.drain()
            e.tensor_tensor(out=stat[:, i, 2:3], in0=stat[:, i, 1:2],
                            in1=stat[:, i, 2:3], op=ALU.subtract).then_inc(h, 1)
        return v_v1

    def p1_back(i, v_v1):
        bslot, lslot = i % N_UB, i % 4
        h_a2, v_a2 = sem.tick("act")

        @on("scalar")
        def _(e, i=i, v_v1=v_v1, h=h_a2):
            e.wait_ge(sem["vec"], v_v1)
            e.activation(out=stat[:, i, 3:4], in_=stat[:, i, 2:3], func=AF.Ln,
                         bias=cst[:, C_EPS:C_EPS + 1])
            e.drain()
            e.activation(out=stat[:, i, 1:2], in_=stat[:, i, 3:4], func=AF.Exp,
                         scale=-0.5, bias=ZB()).then_inc(h, 1)

        h_v2, v_v2 = sem.tick("vec")

        @on("vector")
        def _(e, i=i, bslot=bslot, lslot=lslot, v_a2=v_a2, h=h_v2):
            e.wait_ge(sem["act"], v_a2)
            if i >= 4:
                e.wait_ge(sem["tlu"], u_tr[i - 4])
            e.tensor_scalar(out=lnut[:, lslot, :], in0=ubuf[:, bslot, :],
                            scalar1=stat[:, i, 0:1], scalar2=stat[:, i, 1:2],
                            op0=ALU.subtract, op1=ALU.mult).then_inc(h, 1)
        lnt_done.append(v_v2)

        h_tr, v_tr = sem.tick("tlu", 64)

        @on("sync")
        def _(e, i=i, lslot=lslot, v_v2=v_v2, h=h_tr):
            e.wait_ge(sem["vec"], v_v2)
            if i >= 1:
                e.wait_ge(sem["tlu"], 64 * i)
            hh, loc = i // 16, (i % 16) * 128
            for p4 in range(4):
                e.dma_start_transpose(
                    lnu_half(hh)[:, 2 * p4:2 * p4 + 2, loc:loc + 128],
                    lnut[:, lslot, 256 * p4:256 * (p4 + 1)]).then_inc(h, 16)
        u_tr.append(v_tr)

    wqs_w = A.alloc("wqs_w", 512 * 4)
    wqs_a = A.alloc("wqs_a", 512 * 4)
    wqs_q = A.alloc("wqs_q", 2 * 512 * 4)
    wqs_b = A.alloc("wqs_b", 3 * 128 * 4)
    wqs_s = A.alloc("wqs_s", 2 * 512 * 2, BF16)
    wqs_o = A.alloc("wqs_o", 512 * 2, BF16)
    wqs_t = A.alloc("wqs_t", 512 * 2, BF16)

    WTILES = []
    for kind, srct in (("g", wg_sh), ("u", wu_sh)):
        for r0, rows in ((0, 128), (128, 128), (256, 96)):
            for kh in range(2):
                WTILES.append((kind, srct, r0, rows, kh * 512, 512, D))
    for k0 in range(0, I_FFN, 512):
        WTILES.append(("d", wd_sh, 0, 128, k0, min(512, I_FFN - k0), I_FFN))

    wq_gq_done = []
    wq_tr = []
    wq_sto = []
    def emit_wq_tile(ti):
        (kind, srct, r0, rows, k0, kn, Ktot) = WTILES[ti]
        NB = kn // 16
        h_lw, v_lw = sem.tick("wqld", 16)

        @on("sync")
        def _(e, srct=srct, r0=r0, rows=rows, k0=k0, kn=kn, ti=ti, h=h_lw):
            if ti >= 1:
                e.wait_ge(sem["gq"], wq_gq_done[ti - 1])
                e.wait_ge(sem["wqld"], 16 * ti)
            else:
                e.wait_ge(sem["ld"], LD_CONST)
            e.dma_start(out=wqs_w[0:rows, 0:kn],
                        in_=srct[r0:r0 + rows, k0:k0 + kn]).then_inc(h, 16)

        h_v1, v_v1 = sem.tick("vec")

        @on("vector")
        def _(e, rows=rows, kn=kn, NB=NB, v_lw=v_lw, h=h_v1):
            e.wait_ge(sem["wqld"], v_lw)
            e.tensor_reduce(out=wqs_b[0:rows, 0:NB],
                            in_=wqs_w[0:rows, 0:kn]
                            .rearrange("p (b k) -> p b k", k=16),
                            axis=AX.X, op=ALU.max,
                            apply_absolute_value=True)
            e.drain()
            bm = wqs_b[0:rows, 0:NB]
            sh = wqs_b[0:rows, 128:128 + NB]
            rs = wqs_b[0:rows, 256:256 + NB]
            e.tensor_scalar(out=sh, in0=bm, scalar1=1.0 / 6.0, scalar2=1e-12,
                            op0=ALU.mult, op1=ALU.max)
            e.drain()
            e.reciprocal(out=rs, in_=sh)
            e.drain()
            e.tensor_scalar(out=sh, in0=sh, scalar1=0.5,
                            scalar2=None, op0=ALU.mult).then_inc(h, 1)

        h_g2, v_g2 = sem.tick("vec")

        @on("vector")
        def _(e, rows=rows, kn=kn, NB=NB, h=h_g2):
            e.drain()
            rs = wqs_b[0:rows, 256:256 + NB].unsqueeze(2) \
                .broadcast_to([rows, NB, 16])
            e.tensor_tensor(out=wqs_a[0:rows, 0:kn]
                            .rearrange("p (b k) -> p b k", k=16),
                            in0=wqs_w[0:rows, 0:kn]
                            .rearrange("p (b k) -> p b k", k=16),
                            in1=rs, op=ALU.mult).then_inc(h, 1)

        h_a1, v_a1 = sem.tick("act")

        @on("scalar")
        def _(e, rows=rows, kn=kn, v_g2=v_g2, h=h_a1):
            e.wait_ge(sem["vec"], v_g2)
            a = wqs_a[0:rows, 0:kn]
            e.activation(out=wqs_s[0:rows, 0:kn], in_=a, func=AF.Sign, bias=ZB(0, rows))
            e.drain()
            e.activation(out=a, in_=a, func=AF.Abs, bias=ZB(0, rows)).then_inc(h, 1)

        h_g3, v_g3 = sem.tick("gq")

        @on("vector")
        def _(e, rows=rows, kn=kn, NB=NB, ti=ti, v_a1=v_a1, h=h_g3):
            e.wait_ge(sem["act"], v_a1)
            if ti >= 1:
                e.wait_ge(sem["tr"], wq_tr[ti - 1])
            a = wqs_a[0:rows, 0:kn]
            q2v = wqs_q[0:rows, 0:kn]
            lvb = wqs_q[0:rows, 512:512 + kn]
            e._custom_dve(FP4_P1, out=q2v, in0=a, s0=2.5, s1=3.5, imm2=5.0)
            e.drain()
            e._custom_dve(FP4_P2, out=lvb, in0=a, in1=q2v,
                          s0=0.25, s1=0.75, imm2=1.25)
            e.drain()
            e.scalar_tensor_tensor(out=q2v, in0=a, scalar=1.75, in1=lvb,
                                   op0=ALU.is_gt, op1=ALU.add)
            e.drain()
            sh = wqs_b[0:rows, 128:128 + NB].unsqueeze(2) \
                .broadcast_to([rows, NB, 16])
            ss = wqs_s[0:rows, 512:512 + kn]
            e.tensor_tensor(out=ss.rearrange("p (b k) -> p b k", k=16),
                            in0=wqs_s[0:rows, 0:kn]
                            .rearrange("p (b k) -> p b k", k=16),
                            in1=sh, op=ALU.mult)
            e.drain()
            e.tensor_tensor(out=wqs_o[0:rows, 0:kn], in0=q2v, in1=ss,
                            op=ALU.mult).then_inc(h, 1)
        wq_gq_done.append(v_g3)

        NKC = kn // 128
        h_tw, v_tw, p_tw = tr_tick()

        @on("scalar")
        def _(e, rows=rows, kn=kn, NKC=NKC, ti=ti, v_g3=v_g3, h=h_tw, p=p_tw):
            e.wait_ge(sem["gq"], v_g3)
            if p:
                e.wait_ge(sem["tr"], p)
            if ti >= 1:
                e.wait_ge(sem["stw"], 16 * ti)
            e.dma_start_transpose(
                wqs_t[:, 0:NKC * rows].rearrange("p (a b) -> p a b", a=NKC),
                wqs_o[0:rows, 0:kn]).then_inc(h, 16)
        wq_tr.append(v_tw)

        h_sw, v_sw = sem.tick("stw", 16)

        @on("sync")
        def _(e, kind=kind, r0=r0, rows=rows, k0=k0, Ktot=Ktot, NKC=NKC,
              v_tw=v_tw, h=h_sw):
            e.wait_ge(sem["tr"], v_tw)
            base = {"g": 0, "u": WQ_SEC, "d": 2 * WQ_SEC}[kind]
            shard_n = NSH if kind != "d" else 128
            kc0 = k0 // 128
            dstap = ccw_in.ap()[0, base:base + (Ktot // 128) * 128 * shard_n] \
                .rearrange("(p a b) -> p a b", p=128, b=shard_n)[
                    :, kc0:kc0 + NKC, r0:r0 + rows]
            e.dma_start(out=dstap,
                        in_=wqs_t[:, 0:NKC * rows]
                        .rearrange("p (a b) -> p a b", a=NKC)).then_inc(h, 16)
        wq_sto.append(v_sw)
    vv1_list = []
    for i in range(NTF):
        vv1_list.append(p1_front(i))
        if i >= 1:
            p1_back(i - 1, vv1_list[i - 1])
        if 4 <= i < 4 + len(WTILES):
            emit_wq_tile(i - 4)
    p1_back(NTF - 1, vv1_list[-1])
    TR_LNU = sem.val("tlu")
    mark("p1pre")
    P1_LAST_APPLY = lnt_done[-1]
    dbg("lnu", lnuH0[:, :, 0:256], [("tlu", TR_LNU)])
    dbg("rstd", stat[:, :, 0:2], [("tlu", TR_LNU)])

    WQ_STO = wq_sto[-1]
    WQ_CHAIN = wq_gq_done[-1]
    WQ_TR_LAST = wq_tr[-1]
    mark("wq")

    z67 = A.alloc("z67", L * 4, F32, [128, L])
    for sl in range(8):
        h_mm, v_mm = sem.tick("mm")

        @on("tensor")
        def _(e, sl=sl, h=h_mm, _g1=take_bank(6)):
            _w(e, _g1)
            e.wait_ge(sem["wld"], WLD_ALL)
            for i4 in range(4):
                e.wait_ge(sem["tlu"], u_tr[sl * 4 + i4])
            last = None
            for d in range(8):
                mv = lnu_mov(d, sl)
                e.matmul(ps[0:64, 6, :], wsm[:, d, 0:64], mv,
                         start=(d == 0), stop=(d == 7))
                last = e.matmul(ps[64:66, 6, :], wsm[:, d, 64:66], mv,
                                start=(d == 0), stop=(d == 7))
            last.then_inc(h, 1)

        h_ev, v_ev = sem.tick("vec")

        @on("vector")
        def _(e, sl=sl, v_mm=v_mm, h=h_ev):
            e.wait_ge(sem["mm"], v_mm)
            e.tensor_copy(out=z67[0:66, sl * 512:(sl + 1) * 512],
                          in_=ps[0:66, 6, :]).then_inc(h, 1)
        guard_bank(6, "vec", v_ev)
    Z67 = sem.val("vec")
    guard_bank(6, "vec", Z67)
    dbg("z67", z67[0:66, 0:512], [("vec", Z67)])
    mark("z67")

    h_sp, A_SP = sem.tick("act")

    @on("scalar")
    def _(e, h=h_sp):
        e.wait_ge(sem["vec"], Z67)
        r = z67[64:66, 0:L]
        e.activation(out=r, in_=r, func=AF.Exp, bias=cst[64:66, C_AB:C_AB + 1])
        e.drain()
        e.activation(out=r, in_=r, func=AF.Ln, bias=cst[64:66, C_ONE:C_ONE + 1],
                     accum_out=stat[64:66, 0, 3:4]).then_inc(h, 1)


    for i in range(NTF):
        bk = 4 + (i % 2)
        h_mm, v_mm = sem.tick("mm")

        @on("tensor")
        def _(e, i=i, bk=bk, h=h_mm, _g2=take_bank(bk)):
            _w(e, _g2)
            e.wait_ge(sem["act"], A_SP)
            e.transpose(ps[:, bk, 0:2], z67[64:66, i * 128:(i + 1) * 128],
                        ident[64:66, 64:66]).then_inc(h, 1)

        h_cp, v_cp = sem.tick("vec")

        @on("vector")
        def _(e, i=i, bk=bk, v_mm=v_mm, h=h_cp):
            e.wait_ge(sem["mm"], v_mm)
            e.tensor_copy(out=smeta[:, OFF_DTAL + 3 * i:OFF_DTAL + 3 * i + 2],
                          in_=ps[:, bk, 0:2]).then_inc(h, 1)
        guard_bank(bk, "vec", v_cp)
    DTAL = sem.val("vec")
    dbg("dtal", smeta[:, OFF_DTAL:OFF_DTAL + 3 * NTF], [("vec", DTAL)])

    h_s3, v_s3 = sem.tick("stc", 16)

    @on("sync")
    def _(e, h=h_s3):
        e.wait_ge(sem["act"], A_SP)
        e.dma_start(out=cca_in[0, 0:1], in_=stat[65:66, 0, 3:4]).then_inc(h, 16)

    h_cca, V_CCA = sem.tick("cc")
    h_glA, AGA = sem.tick("rba", 16)

    @on("gpsimd")
    def _(e, h=h_cca, hg=h_glA, v=v_s3):
        e.wait_ge(sem["stc"], v)
        e.collective_compute("AllGather", ALU.bypass,
                             replica_groups=[list(range(8))],
                             ins=[cca_in.ap().opt()],
                             outs=[cca_out.ap().opt()]).then_inc(h, 1)
        e.wait_ge(sem["cc"], V_CCA)
        e.dma_start(out=ascal[:, 0:1], in_=cca_out.ap()).then_inc(hg, 16)

    h_ma, v_ma = sem.tick("mm")

    @on("tensor")
    def _(e, h=h_ma, _g3=take_bank(4)):
        _w(e, _g3)
        e.wait_ge(sem["rba"], AGA)
        e.wait_ge(sem["vec"], V_ONES)
        e.matmul(ps[0:1, 4, 0:1], onesb[0:8, 0:1], ascal[:, 0:1],
                 start=True, stop=True).then_inc(h, 1)

    h_va, v_va = sem.tick("vec")

    @on("vector")
    def _(e, v=v_ma, h=h_va):
        e.wait_ge(sem["mm"], v)
        e.tensor_scalar(out=stat[0:1, 1, 3:4], in0=ps[0:1, 4, 0:1],
                        scalar1=-1.0 / (2.0 * 4 * L), scalar2=None, op0=ALU.mult).then_inc(h, 1)
    guard_bank(4, "vec", v_va)

    h_mb, v_mb = sem.tick("mm")

    @on("tensor")
    def _(e, h=h_mb, v=v_va, _g4=take_bank(4)):
        _w(e, _g4)
        e.wait_ge(sem["vec"], v)
        e.matmul(ps[:, 4, 0:1], onesb[0:1, :], stat[0:1, 1, 3:4],
                 start=True, stop=True).then_inc(h, 1)

    h_vb, ASC = sem.tick("vec")

    @on("vector")
    def _(e, h=h_vb, v=v_mb):
        e.wait_ge(sem["mm"], v)
        e.tensor_copy(out=asc[:, 0:1], in_=ps[:, 4, 0:1]).then_inc(h, 1)
    guard_bank(4, "vec", ASC)
    dbg("asc", asc[:, 0:1], [("vec", ASC)])

    xw_done = {}
    x_mm_of_zs = {}

    def emit_xw_loads(zs):
        for d in range(8):
            slot = (zs * 8 + d) % 8
            h, v = sem.tick("xwld", 16)

            @on("gpsimd")
            def _(e, zs=zs, d=d, slot=slot, h=h):
                if zs >= 1 and d == 0:
                    e.wait_ge(sem["mm"], x_mm_of_zs[zs - 1])
                    e.wait_ge(sem["xwld"], 128 * zs)
                e.dma_start(out=wmv[:, slot, :],
                            in_=win[d * 128:(d + 1) * 128,
                                    zs * 512:(zs + 1) * 512]).then_inc(h, 16)
            xw_done[(zs, d)] = (128 * (zs + 1), slot)

    emit_xw_loads(0)

    mark("p2")

    bc_tok = A.alloc("bc_tok", NTF * 256 * 4, F32, [128, NTF, 256])
    for i in range(NTF):
        bk = i % 2
        h_mm, v_mm = sem.tick("mm")

        @on("tensor")
        def _(e, i=i, bk=bk, h=h_mm, _g5=take_bank(bk)):
            _w(e, _g5)
            e.wait_ge(sem["wld"], WLD_ALL)
            e.wait_ge(sem["tlu"], u_tr[i])
            last = None
            for d in range(8):
                last = e.matmul(ps[:, bk, 0:257], lnu_stat(d, i),
                                wbc[:, d, :], start=(d == 0), stop=(d == 7))
            last.then_inc(h, 1)

        h_sq, v_sq = sem.tick("act")

        @on("scalar")
        def _(e, i=i, bk=bk, v_mm=v_mm, h=h_sq, _g6=take_buf("ascr")):
            e.wait_ge(sem["mm"], v_mm)
            _w(e, _g6)
            e.activation(out=ascr[:, 0:128], in_=ps[:, bk, 0:128],
                         func=AF.Square, bias=ZB(),
                         accum_out=smeta[:, OFF_SSQ + i:OFF_SSQ + i + 1])
            e.activation(out=ascr[:, 128:256], in_=ps[:, bk, 128:256],
                         func=AF.Square, bias=ZB(),
                         accum_out=smeta[:, OFF_SSQ + NTF + i:
                                         OFF_SSQ + NTF + i + 1]).then_inc(h, 1)
        guard_buf("ascr", "act", v_sq)

        h_cp, v_cp = sem.tick("vec")

        @on("vector")
        def _(e, i=i, bk=bk, v_sq=v_sq, h=h_cp):
            e.wait_ge(sem["act"], v_sq)
            e.tensor_copy(out=bc_tok[:, i, :], in_=ps[:, bk, 0:256])
            e.tensor_copy(out=smeta[:, OFF_DTAL + 3 * i + 2:OFF_DTAL + 3 * i + 3],
                          in_=ps[:, bk, 256:257]).then_inc(h, 1)
        guard_bank(bk, "vec", v_cp)
    BC_RAW = sem.val("vec")

    lamv = smeta[:, OFF_DTAL:OFF_DTAL + 3 * NTF] \
        .rearrange("p (t r) -> p r t", r=3)[:, 2, :]
    h_lams, LAMS_A = sem.tick("act")

    @on("scalar")
    def _(e, h=h_lams):
        e.wait_ge(sem["vec"], BC_RAW)
        e.activation(out=lamv, in_=lamv, func=AF.Exp, scale=-1.0,
                     bias=ZB()).then_inc(h, 1)

    h_lamv, LAMV = sem.tick("vec")

    @on("vector")
    def _(e, h=h_lamv):
        e.wait_ge(sem["act"], LAMS_A)
        e.tensor_scalar(out=lamv, in0=lamv, scalar1=1.0, scalar2=None,
                        op0=ALU.add)
        e.drain()
        e.reciprocal(out=lamv, in_=lamv).then_inc(h, 1)
    SSQ = sem.val("act")

    h_rsq, RSQ = sem.tick("act")

    @on("scalar")
    def _(e, h=h_rsq):
        e.wait_ge(sem["act"], SSQ)
        r = smeta[:, OFF_SSQ:OFF_SSQ + 2 * NTF]
        e.activation(out=r, in_=r, func=AF.Ln, scale=1.0 / 128.0,
                     bias=cst[:, C_EPS:C_EPS + 1])
        e.drain()
        e.activation(out=r, in_=r, func=AF.Exp, scale=-0.5,
                     bias=ZB()).then_inc(h, 1)

    h_bn, BC_NORM = sem.tick("vec")

    @on("vector")
    def _(e, h=h_bn):
        e.wait_ge(sem["vec"], BC_RAW)
        e.wait_ge(sem["act"], RSQ)
        v3 = bc_tok[:].rearrange("p t (c k) -> p t c k", c=2)
        rsq = smeta[:, OFF_SSQ:OFF_SSQ + 2 * NTF] \
            .rearrange("p (c t) -> p t c", c=2) \
            .unsqueeze(3).broadcast_to([128, NTF, 2, 128])
        e.tensor_tensor(out=v3, in0=v3, in1=rsq, op=ALU.mult)
        e.drain()
        wv = vbc[:, 0:256].rearrange("p (c k) -> p c k", c=2) \
            .unsqueeze(1).broadcast_to([128, NTF, 2, 128])
        e.tensor_tensor(out=v3, in0=v3, in1=wv, op=ALU.mult)
        e.drain()
        bv = vbc[:, 256:512].rearrange("p (c k) -> p c k", c=2) \
            .unsqueeze(1).broadcast_to([128, NTF, 2, 128])
        e.tensor_tensor(out=v3, in0=v3, in1=bv, op=ALU.add).then_inc(h, 1)
    dbg("bc_norm", bc_tok[:, 0:2, :], [("vec", BC_NORM)])
    mark("p2b")

    h_sel, LNUH = sem.tick("vec")

    @on("vector")
    def _(e, h=h_sel):
        e.wait_ge(sem["vec"], BC_RAW)
        e.wait_ge(sem["vec"], Z67)
        e.wait_ge(sem["ld"], LD_CONST)
        e.tensor_scalar(out=lnuH0[:], in0=lnuH0[:],
                        scalar1=cst[:, C_M0:C_M0 + 1], scalar2=None, op0=ALU.mult)
        e.drain()
        e.scalar_tensor_tensor(out=lnuH0[:], in0=lnuH1[:],
                               scalar=cst[:, C_M1:C_M1 + 1], in1=lnuH0[:],
                               op0=ALU.mult, op1=ALU.add).then_inc(h, 1)
    A.free("lnuH1")

    cum = A.alloc("cum", L * 4, F32, [128, L])
    for sl in range(8):
        h_mm, v_mm = sem.tick("mm")

        @on("tensor")
        def _(e, sl=sl, h=h_mm, _g7=take_bank(5)):
            _w(e, _g7)
            e.wait_ge(sem["act"], A_SP)
            e.wait_ge(sem["vec"], V_ONES)
            e.matmul(ps[0:64, 5, :], onesb[64:65, 0:64],
                     z67[64:65, sl * 512:(sl + 1) * 512],
                     start=True, stop=True).then_inc(h, 1)

        h_da, v_da = sem.tick("vec")

        @on("vector")
        def _(e, sl=sl, v_mm=v_mm, h=h_da):
            e.wait_ge(sem["mm"], v_mm)
            r = z67[0:64, sl * 512:(sl + 1) * 512]
            e.tensor_tensor(out=r, in0=r, in1=ps[0:64, 5, :],
                            op=ALU.mult).then_inc(h, 1)
        guard_bank(5, "vec", v_da)
    DANGLE = sem.val("vec")

    h_cum, CUM = sem.tick("vec")

    @on("vector")
    def _(e, h=h_cum):
        e.wait_ge(sem["vec"], DANGLE)
        e.tensor_tensor_scan(out=cum[0:64, :],
                             data0=onesb[0:64, 0:1].broadcast_to([64, L]),
                             data1=z67[0:64, 0:L], initial=0.0,
                             op0=ALU.mult, op1=ALU.add).then_inc(h, 1)
    dbg("cumang", cum[0:64, 0:512], [("vec", CUM)])

    A.free("z67") if False else None
    angt = A.alloc("angt", NTF * TH * 4, F32, [128, NTF * TH])
    for i in range(NTF):
        bk = 4 + (i % 2)
        h_mm, v_mm = sem.tick("mm")

        @on("tensor")
        def _(e, i=i, bk=bk, h=h_mm, _g8=take_bank(bk)):
            _w(e, _g8)
            e.wait_ge(sem["vec"], CUM)
            e.transpose(ps[:, bk, 0:64], cum[0:64, i * 128:(i + 1) * 128],
                        ident[0:64, 0:64]).then_inc(h, 1)

        h_cp, v_cp = sem.tick("vec")

        @on("vector")
        def _(e, i=i, bk=bk, v_mm=v_mm, h=h_cp):
            e.wait_ge(sem["mm"], v_mm)
            e.tensor_copy(out=angt[:, i * TH:(i + 1) * TH],
                          in_=ps[:, bk, 0:64]).then_inc(h, 1)
        guard_bank(bk, "vec", v_cp)
    ANGT = sem.val("vec")

    trig = A.alloc("trig", NTF * 2 * TH * 4, F32, [128, NTF, 2, TH])
    tsc = A.alloc("tsc", 2 * LH * 4, F32, [128, 2 * LH])

    h_trv, TRV = sem.tick("vec")

    @on("vector")
    def _(e, h=h_trv):
        e.wait_ge(sem["vec"], ANGT)
        a = angt[:]
        k = tsc[:, 0:LH]
        r = tsc[:, LH:2 * LH]
        e.tensor_scalar(out=k, in0=a, scalar1=_INV2PI, scalar2=_MAGIC,
                        op0=ALU.mult, op1=ALU.add)
        e.drain()
        e.tensor_scalar(out=k, in0=k, scalar1=_MAGIC, scalar2=None, op0=ALU.subtract)
        e.drain()
        e.cody_waite_cascade(r, a, k, _CW1, _CW2, _CW3)
        e.drain()
        e.sem_inc(h, 1)

    h_tra1, TRA1 = sem.tick("act")

    @on("scalar")
    def _(e, h=h_tra1):
        e.wait_ge(sem["vec"], TRV)
        r = tsc[:, LH:2 * LH].rearrange("p (t k) -> p t k", k=TH)
        e.activation(out=trig[:, :, 1, :], in_=r, func=AF.Sin,
                     bias=ZB()).then_inc(h, 1)

    h_trv2, TRV2 = sem.tick("vec")

    @on("vector")
    def _(e, h=h_trv2):
        e.wait_ge(sem["act"], TRA1)
        a = angt[:]
        k = tsc[:, 0:LH]
        r = tsc[:, LH:2 * LH]
        e.tensor_scalar(out=k, in0=a, scalar1=_INV2PI, scalar2=_MAGIC + 0.25,
                        op0=ALU.mult, op1=ALU.add)
        e.drain()
        e.tensor_scalar(out=k, in0=k, scalar1=_MAGIC, scalar2=None, op0=ALU.subtract)
        e.drain()
        e.cody_waite_cascade(r, a, k, _CW1, _CW2, _CW3)
        e.drain()
        e.sem_inc(h, 1)

    h_tra2, TRIG = sem.tick("act")

    @on("scalar")
    def _(e, h=h_tra2):
        e.wait_ge(sem["vec"], TRV2)
        r = tsc[:, LH:2 * LH].rearrange("p (t k) -> p t k", k=TH)
        e.activation(out=trig[:, :, 0, :], in_=r, func=AF.Sin,
                     bias=cst[:, C_HPI:C_HPI + 1]).then_inc(h, 1)
    dbg("trig", trig[:, 0:2, :, :], [("act", TRIG)])
    mark("p2c")

    for zs in range(4):
        if zs >= 1:
            emit_xw_loads(zs)
        for i in range(NT):
            bk = i % 2
            h_mm, v_mm = sem.tick("mm")

            @on("tensor")
            def _(e, zs=zs, i=i, bk=bk, h=h_mm, _g9=take_bank(bk)):
                _w(e, _g9)
                e.wait_ge(sem["vec"], LNUH)
                last = None
                e.wait_ge(sem["xwld"], xw_done[(zs, 0)][0])
                for d in range(8):
                    g = xw_done[(zs, d)]
                    last = e.matmul(ps[:, bk, :],
                                    lnuH0[:, d, i * 128:(i + 1) * 128],
                                    wmv[:, g[1], :], start=(d == 0),
                                    stop=(d == 7))
                last.then_inc(h, 1)
            x_mm_of_zs[zs] = v_mm

            h_sx, v_sx = sem.tick("act")

            @on("scalar")
            def _(e, zs=zs, i=i, bk=bk, v_mm=v_mm, h=h_sx, _g10=take_buf("xsil")):
                e.wait_ge(sem["mm"], v_mm)
                _w(e, _g10)
                e.activation(out=xsil[:], in_=ps[:, bk, :], func=AF.Silu,
                             bias=ZB(),
                             accum_out=mx4[:, i, zs:zs + 1]).then_inc(h, 1)
            guard_buf("xsil", "act", v_sx)
            guard_bank(bk, "act", v_sx)
    MX4 = sem.val("act")

    h_mx, MXTOK = sem.tick("vec")

    @on("vector")
    def _(e, h=h_mx):
        e.wait_ge(sem["act"], MX4)
        r = smeta[:, OFF_MX:OFF_MX + NT]
        e.tensor_reduce(out=r, in_=mx4[:], axis=AX.X, op=ALU.add)
        e.drain()
        e.tensor_scalar(out=r, in0=r, scalar1=1.0 / E, scalar2=None, op0=ALU.mult).then_inc(h, 1)

    h_mxt, v_mxt = sem.tick("mm")

    @on("tensor")
    def _(e, h=h_mxt, _g11=take_bank(4)):
        _w(e, _g11)
        e.wait_ge(sem["vec"], MXTOK)
        e.transpose(ps[0:NT, 4, 0:128], smeta[:, OFF_MX:OFF_MX + NT],
                    ident[:]).then_inc(h, 1)

    h_mxc, v_mxc = sem.tick("vec")

    @on("vector")
    def _(e, v=v_mxt, h=h_mxc):
        e.wait_ge(sem["mm"], v)
        e.tensor_copy(out=mxr[:], in_=ps[0:NT, 4, 0:128]).then_inc(h, 1)
    guard_bank(4, "vec", v_mxc)

    h_p1, v_p1 = sem.tick("stm", 16)

    @on("sync")
    def _(e, h=h_p1, v=v_mxc):
        e.wait_ge(sem["vec"], v)
        e.dma_start(out=ccs_in[0, 0:LH], in_=mxr[:]).then_inc(h, 16)

    h_ccw, V_CCW = sem.tick("cc")
    h_ccm, V_CCM = sem.tick("cc")

    @on("gpsimd")
    def _(e, hw=h_ccw, hm=h_ccm, vw=WQ_STO, vm=v_p1):
        e.wait_ge(sem["stw"], vw)
        e.collective_compute("AllGather", ALU.bypass,
                             replica_groups=[list(range(8))],
                             ins=[ccw_in.ap().opt()],
                             outs=[ccw_out.ap().opt()]).then_inc(hw, 1)
        e.wait_ge(sem["stm"], vm)
        e.collective_compute("AllGather", ALU.bypass,
                             replica_groups=[[0, 1], [2, 3], [4, 5], [6, 7]],
                             ins=[ccs_in.ap().opt()],
                             outs=[ccs_out.ap().opt()]).then_inc(hm, 1)

    A.free("wqs_w"); A.free("wqs_a"); A.free("wqs_q"); A.free("wqs_b")
    A.free("wqs_s"); A.free("wqs_o"); A.free("wqs_t")
    A.free("ubuf")
    mxsb = A.alloc("mxsb", (L + 128) * 4, F32, [128, L + 128])
    MXSB_FENCE = [("gq", WQ_CHAIN), ("tr", WQ_TR_LAST), ("stw", WQ_STO),
                  ("vec", P1_LAST_APPLY)]
    h_mz, MXZ = sem.tick("vec")

    @on("vector")
    def _(e, h=h_mz):
        for s, vv in MXSB_FENCE:
            e.wait_ge(sem[s], vv)
        e.memset(mxsb[0:1, 0:64], 0.0).then_inc(h, 1)

    h_m1, v_m1 = sem.tick("rbm", 16)
    h_m2, v_m2 = sem.tick("rbm", 16)

    @on("sync")
    def _(e, h1=h_m1, h2=h_m2, vz=MXZ):
        e.wait_ge(sem["cc"], V_CCM)
        e.wait_ge(sem["vec"], vz)
        for s, vv in MXSB_FENCE:
            e.wait_ge(sem[s], vv)
        e.dma_start(out=mxsb[0:1, 64:64 + LH],
                    in_=ccs_out.ap()[0:1, 0:LH]).then_inc(h1, 16)
        e.dma_start(out=mxsb[0:1, 64 + LH:64 + L],
                    in_=ccs_out.ap()[1:2, 0:LH]).then_inc(h2, 16)

    for i in range(NTF):
        bk = 4 + (i % 2)
        h_mm, v_mm = sem.tick("mm")

        @on("tensor")
        def _(e, i=i, bk=bk, h=h_mm, _g12=take_bank(bk)):
            _w(e, _g12)
            e.wait_ge(sem["rbm"], v_m2)
            e.transpose(ps[:, bk, 0:1],
                        mxsb[0:1, 64 + i * 128:64 + (i + 1) * 128],
                        ident[0:1, 0:1])
            e.transpose(ps[:, bk, 1:2],
                        mxsb[0:1, 63 + i * 128:63 + (i + 1) * 128],
                        ident[0:1, 0:1]).then_inc(h, 1)

        h_cp, v_cp = sem.tick("vec")

        @on("vector")
        def _(e, i=i, bk=bk, v_mm=v_mm, h=h_cp):
            e.wait_ge(sem["mm"], v_mm)
            e.tensor_copy(out=smeta[:, OFF_MXF + 2 * i:OFF_MXF + 2 * i + 2],
                          in_=ps[:, bk, 0:2]).then_inc(h, 1)
        guard_bank(bk, "vec", v_cp)
    MXTOKF = sem.val("vec")
    A.free("mxsb")
    mark("p3")

    h_um1, UM1 = sem.tick("vec")

    @on("vector")
    def _(e, h=h_um1):
        e.wait_ge(sem["vec"], MXTOKF)
        e.wait_ge(sem["vec"], DTAL)
        e.wait_ge(sem["vec"], LAMV)
        r3 = smeta[:, OFF_DTAL:OFF_DTAL + 3 * NTF] \
            .rearrange("p (t r) -> p r t", r=3)
        t1 = smeta[:, OFF_ALPHA:OFF_ALPHA + NTF]
        e.tensor_tensor(out=t1, in0=r3[:, 0, :], in1=r3[:, 1, :],
                        op=ALU.mult).then_inc(h, 1)

    h_um2, UM2 = sem.tick("act")

    @on("scalar")
    def _(e, h=h_um2):
        e.wait_ge(sem["vec"], UM1)
        t1 = smeta[:, OFF_ALPHA:OFF_ALPHA + NTF]
        e.activation(out=t1, in_=t1, func=AF.Exp, scale=-1.0,
                     bias=ZB()).then_inc(h, 1)

    h_um3, UM3 = sem.tick("vec")

    @on("vector")
    def _(e, h=h_um3):
        e.wait_ge(sem["act"], UM2)
        r3 = smeta[:, OFF_DTAL:OFF_DTAL + 3 * NTF] \
            .rearrange("p (t r) -> p r t", r=3)
        dt, lam = r3[:, 0, :], r3[:, 2, :]
        r2 = smeta[:, OFF_MXF:OFF_MXF + 2 * NTF] \
            .rearrange("p (t r) -> p r t", r=2)
        mx, mxp = r2[:, 0, :], r2[:, 1, :]
        alpha = smeta[:, OFF_ALPHA:OFF_ALPHA + NTF]
        um = smeta[:, OFF_UM:OFF_UM + NTF]
        dtu = smeta[:, OFF_DTU:OFF_DTU + NTF]
        tmp = smeta[:, OFF_TMP:OFF_TMP + NTF]
        e.tensor_tensor(out=um, in0=alpha, in1=mxp, op=ALU.mult)
        e.drain()
        e.scalar_tensor_tensor(out=tmp, in0=lam, scalar=-1.0, in1=um,
                               op0=ALU.mult, op1=ALU.mult)
        e.drain()
        e.tensor_tensor(out=um, in0=um, in1=tmp, op=ALU.add)
        e.drain()
        e.tensor_tensor(out=tmp, in0=lam, in1=mx, op=ALU.mult)
        e.drain()
        e.tensor_tensor(out=um, in0=um, in1=tmp, op=ALU.add)
        e.drain()
        e.tensor_tensor(out=dtu, in0=dt, in1=um, op=ALU.mult).then_inc(h, 1)
    dbg("umean", smeta[:, OFF_UM:OFF_UM + NTF], [("vec", UM3)])
    mark("p4a")

    h_rp, ROPE = sem.tick("vec")

    @on("vector")
    def _(e, h=h_rp):
        e.wait_ge(sem["vec"], BC_NORM)
        e.wait_ge(sem["act"], TRIG)
        e.wait_ge(sem["vec"], UM3)
        cos = trig[:, :, 0, :]
        sin = trig[:, :, 1, :]
        m1 = tsc[:, 0:LH].rearrange("p (t k) -> p t k", k=TH)
        m2 = tsc[:, LH:2 * LH].rearrange("p (t k) -> p t k", k=TH)
        for c in range(2):
            lo = bc_tok[:, :, c * 128:c * 128 + TH]
            hi = bc_tok[:, :, c * 128 + TH:c * 128 + 128]
            e.tensor_tensor(out=m1, in0=lo, in1=cos, op=ALU.mult)
            e.tensor_tensor(out=m2, in0=hi, in1=sin, op=ALU.mult)
            e.drain()
            e.tensor_tensor(out=m2, in0=m1, in1=m2, op=ALU.subtract)
            e.drain()
            e.tensor_tensor(out=m1, in0=lo, in1=sin, op=ALU.mult)
            e.drain()
            e.tensor_copy(out=lo, in_=m2)
            e.drain()
            e.tensor_tensor(out=m2, in0=hi, in1=cos, op=ALU.mult)
            e.drain()
            e.tensor_tensor(out=hi, in0=m1, in1=m2, op=ALU.add)
            e.drain()
        bpart = bc_tok[:].rearrange("p t (c k) -> p t c k", c=2)[:, :, 0, :]
        dtub = smeta[:, OFF_DTU:OFF_DTU + NTF].unsqueeze(2) \
            .broadcast_to([128, NTF, 128])
        e.tensor_tensor(out=bpart, in0=bpart, in1=dtub,
                        op=ALU.mult).then_inc(h, 1)
    dbg("rope", bc_tok[:, 0:2, :], [("vec", ROPE)])

    A.free("trig")
    A.free("angt")
    A.free("tsc")
    dbu = A.alloc("dbu", L * 4, F32, [128, L])
    ctm = A.alloc("ctm", L * 4, F32, [128, L])
    for i in range(NTF):
        bkB, bkC = 4 + (i % 2), 6 + (i % 2)
        h_mm, v_mm = sem.tick("mm")

        @on("tensor")
        def _(e, i=i, bkB=bkB, bkC=bkC, h=h_mm, _g13=take_bank(bkB), _g14=take_bank(bkC)):
            _w(e, _g13)
            _w(e, _g14)
            e.wait_ge(sem["vec"], ROPE)
            e.transpose(ps[:, bkB, 0:128], bc_tok[:, i, 0:128], ident[:])
            e.transpose(ps[:, bkC, 0:128], bc_tok[:, i, 128:256],
                        ident[:]).then_inc(h, 1)

        h_cp, v_cp = sem.tick("vec")

        @on("vector")
        def _(e, i=i, bkB=bkB, bkC=bkC, v_mm=v_mm, h=h_cp):
            e.wait_ge(sem["mm"], v_mm)
            e.tensor_copy(out=dbu[:, i * 128:(i + 1) * 128], in_=ps[:, bkB, 0:128])
            e.tensor_copy(out=ctm[:, i * 128:(i + 1) * 128],
                          in_=ps[:, bkC, 0:128]).then_inc(h, 1)
        guard_bank(bkB, "vec", v_cp)
        guard_bank(bkC, "vec", v_cp)
    TR_BC = sem.val("vec")

    A.free("bc_tok")
    dabc = A.alloc("dabc", L * 4, F32, [128, L])
    dab_acts = []
    for sl in range(8):
        bk = 4 + (sl % 2)
        h_mm, v_mm = sem.tick("mm")

        @on("tensor")
        def _(e, sl=sl, bk=bk, h=h_mm, _g15=take_bank(bk)):
            _w(e, _g15)
            e.wait_ge(sem["act"], A_SP)
            e.wait_ge(sem["vec"], ASC)
            e.matmul(ps[:, bk, :], onesb[64:65, :],
                     z67[64:65, sl * 512:(sl + 1) * 512],
                     start=True, stop=True).then_inc(h, 1)

        h_cp, v_cp = sem.tick("act")

        @on("scalar")
        def _(e, sl=sl, bk=bk, v_mm=v_mm, h=h_cp):
            e.wait_ge(sem["mm"], v_mm)
            e.wait_ge(sem["vec"], ROPE)
            e.activation(out=dabc[:, sl * 512:(sl + 1) * 512],
                         in_=ps[:, bk, :], func=AF.Exp, scale=asc[:, 0:1],
                         bias=ZB()).then_inc(h, 1)
        guard_bank(bk, "act", v_cp)
        dab_acts.append(v_cp)
    DABC_A = dab_acts[-1]
    A.free("z67")

    states = A.alloc("states", L * 4, F32, [128, L])
    h_scan, SCAN = sem.tick("vec")

    @on("vector")
    def _(e, h=h_scan):
        e.wait_ge(sem["act"], DABC_A)
        e.wait_ge(sem["vec"], TR_BC)
        e.tensor_tensor_scan(out=states[:], data0=dabc[:], data1=dbu[:],
                             initial=0.0, op0=ALU.mult, op1=ALU.add)
        e.drain()
        e.tensor_tensor(out=ctm[:], in0=states[:], in1=ctm[:],
                        op=ALU.mult).then_inc(h, 1)
    dbg("states", states[:, 0:512], [("vec", SCAN)])

    for sl in range(8):
        bk = 4 + (sl % 2)
        h_mm, v_mm = sem.tick("mm")

        @on("tensor")
        def _(e, sl=sl, bk=bk, h=h_mm, _g16=take_bank(bk)):
            _w(e, _g16)
            e.wait_ge(sem["vec"], SCAN)
            e.matmul(ps[0:1, bk, :], onesb[:, 0:1],
                     ctm[:, sl * 512:(sl + 1) * 512],
                     start=True, stop=True).then_inc(h, 1)

        h_cp, v_cp = sem.tick("vec")

        @on("vector")
        def _(e, sl=sl, bk=bk, v_mm=v_mm, h=h_cp):
            e.wait_ge(sem["mm"], v_mm)
            e.tensor_copy(out=dabc[0:1, sl * 512:(sl + 1) * 512],
                          in_=ps[0:1, bk, :]).then_inc(h, 1)
        guard_bank(bk, "vec", v_cp)
    YROW = sem.val("vec")

    for i in range(NTF):
        bk = 4 + (i % 2)
        h_mm, v_mm = sem.tick("mm")

        @on("tensor")
        def _(e, i=i, bk=bk, h=h_mm, _g17=take_bank(bk)):
            _w(e, _g17)
            e.wait_ge(sem["vec"], YROW)
            e.transpose(ps[:, bk, 0:1],
                        dabc[0:1, i * 128:(i + 1) * 128],
                        ident[0:1, 0:1]).then_inc(h, 1)

        h_cp, v_cp = sem.tick("vec")

        @on("vector")
        def _(e, i=i, bk=bk, v_mm=v_mm, h=h_cp):
            e.wait_ge(sem["mm"], v_mm)
            e.tensor_copy(out=smeta[:, OFF_YTOK + i:OFF_YTOK + i + 1],
                          in_=ps[:, bk, 0:1]).then_inc(h, 1)
        guard_bank(bk, "vec", v_cp)
    YTOKV = sem.val("vec")

    h_yo, YOWN = sem.tick("vec")

    @on("vector")
    def _(e, h=h_yo):
        e.wait_ge(sem["vec"], YTOKV)
        yf = smeta[:, OFF_YTOK:OFF_YTOK + NTF]
        yo = smeta[:, OFF_YOWN:OFF_YOWN + NT]
        e.tensor_scalar(out=yo, in0=yf[:, 0:NT], scalar1=cst[:, C_M0:C_M0 + 1],
                        scalar2=None, op0=ALU.mult)
        e.drain()
        e.scalar_tensor_tensor(out=yo, in0=yf[:, NT:NTF],
                               scalar=cst[:, C_M1:C_M1 + 1], in1=yo,
                               op0=ALU.mult, op1=ALU.add).then_inc(h, 1)
    dbg("yown", smeta[:, OFF_YOWN:OFF_YOWN + NT], [("vec", YOWN)])
    mark("p4")

    A.free("cum")
    A.free("dbu")
    A.free("ctm")
    A.free("dabc")
    A.free("states")
    A.free("wmv")
    A.free("xsil")
    wgt = A.alloc("wgt", 32 * 512 * 2, BF16, [128, 32, 512])
    wob = A.alloc("wob", 16 * D * 2, BF16, [128, 16, D])
    gtok = A.alloc("gtok", E * 2, BF16, [128, E])
    ygT = A.alloc("ygT", 2 * 16 * 128 * 2, BF16, [128, 2, 16, 128])
    x1t = A.alloc("x1t", 4 * D * 4, F32, [128, 4, D])
    urest = A.alloc("urest", 2 * D * 4, F32, [128, 2, D])
    qsc = A.alloc("qsc", (3 * D + 128) * 4, F32, [128, 3 * D + 128])
    xqt = A.alloc("xqt", 2 * D * 2, BF16, [128, 2, D])
    xqs = A.alloc("xqs", 2 * 8 * 128 * 2, BF16, [128, 2, 8, 128])
    qbf = A.alloc("qbf", 2 * D * 2, BF16, [128, 2 * D])

    WGT_FENCE = [("vec", LNUH), ("vec", CUM), ("vec", DANGLE),
                 ("gq", WQ_CHAIN), ("tr", WQ_TR_LAST), ("stw", WQ_STO),
                 ("vec", P1_LAST_APPLY), ("act", A_SP), ("vec", LAMV),
                 ("vec", YROW), ("vec", TR_BC), ("vec", YTOKV),
                 ("vec", ROPE), ("vec", SCAN), ("act", DABC_A)]
    wgt_done = []
    for zs in range(4):
        for d in range(8):
            h, v = sem.tick("gwld", 16)

            @on("gpsimd")
            def _(e, zs=zs, d=d, h=h, first=(zs == 0 and d == 0)):
                if first:
                    for s, vv in WGT_FENCE:
                        e.wait_ge(sem[s], vv)
                e.dma_start(out=wgt[:, zs * 8 + d, :],
                            in_=win[d * 128:(d + 1) * 128,
                                    E + zs * 512:E + (zs + 1) * 512]).then_inc(h, 16)
            wgt_done.append(v)
    WGT = wgt_done[-1]
    wob_done = []
    for ec in range(16):
        h, v = sem.tick("wobld", 16)

        @on("gpsimd")
        def _(e, ec=ec, h=h):
            if ec == 0:
                e.wait_ge(sem["gwld"], WGT)
            e.dma_start(out=wob[:, ec, :],
                        in_=wout[ec * 128:(ec + 1) * 128, :]).then_inc(h, 16)
        wob_done.append(v)
    WOB = wob_done[-1]

    s2_done = []
    x1_done = []
    xq_tr = []
    yg_tr = []
    xqs_st = []
    xq_gq = []
    xq_vq3 = []

    def p5_front(i):
        gsl = i % 2
        h_gm, v_gm = sem.tick("mm")

        @on("tensor")
        def _(e, i=i, h=h_gm, _g18=[take_bank(b) for b in range(4)]):
            for g in _g18:
                _w(e, g)
            e.wait_ge(sem["gwld"], WGT)
            last = None
            for zs in range(4):
                for d in range(8):
                    last = e.matmul(ps[:, zs, :],
                                    lnuH0[:, d, i * 128:(i + 1) * 128],
                                    wgt[:, zs * 8 + d, :],
                                    start=(d == 0), stop=(d == 7))
            last.then_inc(h, 1)

        h_gs, v_gs = sem.tick("act")

        @on("scalar")
        def _(e, i=i, v_gm=v_gm, h=h_gs):
            e.wait_ge(sem["mm"], v_gm)
            if i >= 1:
                e.wait_ge(sem["tyg"], 64 * i)
            last = None
            for zs in range(4):
                last = e.activation(out=gtok[:, zs * 512:(zs + 1) * 512],
                                    in_=ps[:, zs, :], func=AF.Silu, bias=ZB())
            last.then_inc(h, 1)
        for b4 in range(4):
            guard_bank(b4, "act", v_gs)

        h_yg, v_yg = sem.tick("vec")

        @on("vector")
        def _(e, i=i, v_gs=v_gs, h=h_yg):
            e.wait_ge(sem["act"], v_gs)
            e.wait_ge(sem["vec"], YOWN)
            g = gtok[:]
            e.tensor_scalar(out=g, in0=g,
                            scalar1=smeta[:, OFF_YOWN + i:OFF_YOWN + i + 1],
                            scalar2=None, op0=ALU.mult).then_inc(h, 1)

        h_ty, v_ty = sem.tick("tyg", 64)

        @on("sync")
        def _(e, i=i, gsl=gsl, v_yg=v_yg, h=h_ty):
            e.wait_ge(sem["vec"], v_yg)
            if i >= 1:
                e.wait_ge(sem["tyg"], 64 * i)
            if i >= 2:
                e.wait_ge(sem["mm"], s2_done[i - 2])
            for p4 in range(4):
                e.dma_start_transpose(
                    ygT[:, gsl, 4 * p4:4 * p4 + 4, :],
                    gtok[:, 512 * p4:512 * (p4 + 1)]).then_inc(h, 16)
        yg_tr.append(v_ty)

        h_s2, v_s2b = sem.tick("mm")

        @on("tensor")
        def _(e, i=i, gsl=gsl, v_ty=v_ty, h=h_s2, _g19=take_bank(4), _g20=take_bank(5)):
            _w(e, _g19)
            _w(e, _g20)
            e.wait_ge(sem["tyg"], v_ty)
            e.wait_ge(sem["wobld"], WOB)
            last = None
            for ec in range(16):
                for half in range(2):
                    last = e.matmul(ps[:, 4 + half, :], ygT[:, gsl, ec, :],
                                    wob[:, ec, half * 512:(half + 1) * 512],
                                    start=(ec == 0), stop=(ec == 15))
            last.then_inc(h, 1)
        s2_done.append(v_s2b)

        usem5 = f"url{i % 2}"
        h_ur, v_ur = sem.tick(usem5, 16)

        @on("sync")
        def _(e, i=i, h=h_ur, usem5=usem5):
            if i >= 2:
                e.wait_ge(sem["vec"], x1_done[i - 2])
                e.wait_ge(sem[usem5], 16 * (i // 2))
            else:
                for s, vv in WGT_FENCE:
                    e.wait_ge(sem[s], vv)
            e.dma_start(out=urest[:, i % 2, :],
                        in_=uh_in[i * 128:(i + 1) * 128, :]).then_inc(h, 16)

        h_x1, v_x1 = sem.tick("vec")

        @on("vector")
        def _(e, i=i, v_s2b=v_s2b, v_ur=v_ur, usem5=usem5, h=h_x1):
            e.wait_ge(sem["mm"], v_s2b)
            e.wait_ge(sem[usem5], v_ur)
            if i >= 2:
                e.wait_ge(sem[f"stx{i % 2}"], 16 * (i // 2))
            last = None
            for half in range(2):
                last = e.tensor_tensor(
                    out=x1t[:, i % 4, half * 512:(half + 1) * 512],
                    in0=ps[:, 4 + half, :],
                    in1=urest[:, i % 2, half * 512:(half + 1) * 512],
                    op=ALU.add)
            last.then_inc(h, 1)
        x1_done.append(v_x1)
        guard_bank(4, "vec", v_x1)
        guard_bank(5, "vec", v_x1)

        ssem5 = f"stx{i % 2}"
        h_xs, v_xs = sem.tick(ssem5, 16)

        @on("sync")
        def _(e, i=i, v_x1=v_x1, h=h_xs, ssem5=ssem5):
            e.wait_ge(sem["vec"], v_x1)
            if i >= 2:
                e.wait_ge(sem[ssem5], 16 * (i // 2))
            e.dma_start(out=x1_d[i * 128:(i + 1) * 128, :],
                        in_=x1t[:, i % 4, :]).then_inc(h, 16)

    def p5_back(i):
        v_x1 = x1_done[i]
        h_ls, v_ls = sem.tick("act")

        @on("scalar")
        def _(e, i=i, v_x1=v_x1, h=h_ls, _g21=take_buf("ascr")):
            e.wait_ge(sem["vec"], v_x1)
            _w(e, _g21)
            e.activation(out=ascr[:, 0:D], in_=x1t[:, i % 4, :],
                         func=AF.Identity, scale=-1.0 / D, bias=ZB(),
                         accum_out=smeta[:, OFF_X1M + i:OFF_X1M + i + 1])
            e.activation(out=ascr[:, D:2 * D], in_=x1t[:, i % 4, :],
                         func=AF.Square, scale=1.0 / 32.0, bias=ZB(),
                         accum_out=smeta[:, OFF_RSTD1 + i:
                                         OFF_RSTD1 + i + 1]).then_inc(h, 1)
        guard_buf("ascr", "act", v_ls)

        h_vv, v_vv = sem.tick("vec")

        @on("vector")
        def _(e, i=i, v_ls=v_ls, h=h_vv):
            e.wait_ge(sem["act"], v_ls)
            m = smeta[:, OFF_X1M + i:OFF_X1M + i + 1]
            ssq = smeta[:, OFF_RSTD1 + i:OFF_RSTD1 + i + 1]
            tmp = smeta[:, OFF_TMP:OFF_TMP + 1]
            e.tensor_scalar(out=tmp, in0=m, scalar1=m, scalar2=None,
                            op0=ALU.mult)
            e.drain()
            e.tensor_tensor(out=ssq, in0=ssq, in1=tmp,
                            op=ALU.subtract).then_inc(h, 1)

        h_rs, v_rs = sem.tick("act")

        @on("scalar")
        def _(e, i=i, v_vv=v_vv, h=h_rs,
              vGp=(xq_gq[i - 1] if i >= 1 else None)):
            e.wait_ge(sem["vec"], v_vv)
            if vGp is not None:
                e.wait_ge(sem["vec"], vGp)
            r = smeta[:, OFF_RSTD1 + i:OFF_RSTD1 + i + 1]
            e.activation(out=r, in_=r, func=AF.Ln, bias=cst[:, C_EPS:C_EPS + 1])
            e.drain()
            e.activation(out=r, in_=r, func=AF.Exp, scale=-0.5, bias=ZB())
            e.drain()
            e.activation(out=qsc[:, 2 * D:3 * D], in_=x1t[:, i % 4, :],
                         func=AF.Identity,
                         bias=smeta[:, OFF_X1M + i:OFF_X1M + i + 1])
            e.drain()
            e.activation(out=qbf[:, 0:D], in_=qsc[:, 2 * D:3 * D],
                         func=AF.Sign, bias=ZB()).then_inc(h, 1)

        h_q1, v_q1 = sem.tick("vec")

        @on("vector")
        def _(e, i=i, v_rs=v_rs, h=h_q1):
            e.wait_ge(sem["act"], v_rs)
            t = qsc[:, 2 * D:3 * D]
            bm = qsc[:, 3 * D:3 * D + 64]
            rb = qsc[:, 3 * D + 64:3 * D + 128]
            e.tensor_reduce(out=bm, in_=t.rearrange("p (b k) -> p b k", k=16),
                            axis=AX.X, op=ALU.max, apply_absolute_value=True)
            e.drain()
            e.tensor_scalar(out=rb, in0=bm, scalar1=1e-30, scalar2=None, op0=ALU.max)
            e.drain()
            e.reciprocal(out=rb, in_=rb)
            e.drain()
            e.tensor_scalar(out=bm, in0=bm,
                            scalar1=smeta[:, OFF_RSTD1 + i:OFF_RSTD1 + i + 1],
                            scalar2=1.0 / 12.0, op0=ALU.mult, op1=ALU.mult)
            e.drain()
            e.tensor_tensor(out=qsc[:, D:2 * D].rearrange("p (b k) -> p b k", k=16),
                            in0=t.rearrange("p (b k) -> p b k", k=16),
                            in1=rb.unsqueeze(2).broadcast_to([128, 64, 16]),
                            op=ALU.mult).then_inc(h, 1)

        h_q2, v_q2 = sem.tick("act")

        @on("scalar")
        def _(e, i=i, v_q1=v_q1, h=h_q2):
            e.wait_ge(sem["vec"], v_q1)
            xn = qsc[:, D:2 * D]
            e.activation(out=xn, in_=xn, func=AF.Abs, bias=ZB()).then_inc(h, 1)

        h_q3, v_q3 = sem.tick("vec")

        @on("vector")
        def _(e, i=i, v_q2=v_q2, h=h_q3):
            e.wait_ge(sem["act"], v_q2)
            a = qsc[:, D:2 * D]
            q2v = qsc[:, 0:D]
            lvlb = qsc[:, 2 * D:3 * D]
            e._custom_dve(FP4_P1, out=q2v, in0=a, s0=2.5 / 6.0, s1=3.5 / 6.0,
                          imm2=5.0 / 6.0)
            e.drain()
            e._custom_dve(FP4_P2, out=lvlb, in0=a, in1=q2v,
                          s0=0.25 / 6.0, s1=0.75 / 6.0, imm2=1.25 / 6.0)
            e.drain()
            e.scalar_tensor_tensor(out=q2v, in0=a, scalar=1.75 / 6.0, in1=lvlb,
                                   op0=ALU.is_gt, op1=ALU.add).then_inc(h, 1)
        xq_vq3.append(v_q3)

        h_gx, v_gx = sem.tick("vec")

        @on("vector")
        def _(e, i=i, v_q3=v_q3, h=h_gx):
            if i >= 2:
                e.wait_ge(sem["txq"], 32 * (i - 1))
            ss = qsc[:, D:2 * D]
            bms = qsc[:, 3 * D:3 * D + 64]
            e.tensor_tensor(out=ss.rearrange("p (b k) -> p b k", k=16),
                            in0=qbf[:, 0:D].rearrange("p (b k) -> p b k", k=16),
                            in1=bms.unsqueeze(2).broadcast_to([128, 64, 16]),
                            op=ALU.mult)
            e.drain()
            e.tensor_tensor(out=xqt[:, i % 2, :], in0=qsc[:, 0:D], in1=ss,
                            op=ALU.mult).then_inc(h, 1)
        xq_gq.append(v_gx)

        h_tx, v_tx = sem.tick("txq", 32)

        @on("sync")
        def _(e, i=i, v_gx=v_gx, h=h_tx):
            e.wait_ge(sem["vec"], v_gx)
            if i >= 1:
                e.wait_ge(sem["txq"], 32 * i)
            if i >= 2:
                e.wait_ge(sem["xqs"], 16 * (i - 1))
            for p2 in range(2):
                e.dma_start_transpose(
                    xqs[:, i % 2, 4 * p2:4 * p2 + 4, :],
                    xqt[:, i % 2, 512 * p2:512 * (p2 + 1)]).then_inc(h, 16)
        xq_tr.append(v_tx)

        h_xq2, v_xq2 = sem.tick("xqs", 16)

        @on("sync")
        def _(e, i=i, v_tx=v_tx, h=h_xq2):
            e.wait_ge(sem["txq"], v_tx)
            e.dma_start(out=xq_d.ap()[:, :, i * 128:(i + 1) * 128],
                        in_=xqs[:, i % 2, :, :]).then_inc(h, 16)
        xqs_st.append(v_xq2)

    for i in range(NT):
        p5_front(i)
        if i >= 2:
            p5_back(i - 2)
    p5_back(NT - 2)
    p5_back(NT - 1)
    dbg("x1", x1t[:, (NT - 1) % 4, :], [("vec", x1_done[-1])])
    mark("p5")

    P5_FENCE = [("mm", sem.val("mm")), ("vec", sem.val("vec")),
                ("act", sem.val("act")), ("gq", sem.val("gq")),
                ("tyg", sem.val("tyg")), ("txq", sem.val("txq")),
                ("xqs", sem.val("xqs")), ("tr", sem.val("tr")), ("tlu", sem.val("tlu")),
                ("stx0", sem.val("stx0")), ("stx1", sem.val("stx1")),
                ("url0", sem.val("url0")), ("url1", sem.val("url1")),
                ("gwld", WGT), ("wobld", WOB)]
    A.free("lnuH0")
    A.free("wgt")
    A.free("wob")
    A.free("gtok")
    A.free("ygT")
    A.free("urest")
    A.free("qsc")
    A.free("xqt")
    A.free("xqs")
    A.free("qbf")
    A.free("x1t")
    A.free("ascr")
    A.free("lnut")
    wgq = A.alloc("wgq", 8 * I_FFN * 2, BF16, [128, 8, I_FFN])
    wuq = A.alloc("wuq", 8 * I_FFN * 2, BF16, [128, 8, I_FFN])
    wdq = A.alloc("wdq", 22 * 1024 * 2, BF16, [128, 22, 1024])
    xqg = A.alloc("xqg", 2 * 8 * 128 * 2, BF16, [128, 2, 8, 128])
    hsil = A.alloc("hsil", 1024 * 4, F32, [128, 1024])
    hqh = A.alloc("hqh", 2 * 1024 * 4, F32, [128, 2, 1024])
    qff = A.alloc("qff", 2 * 1024 * 4, F32, [128, 2, 1024])
    sgnb = A.alloc("sgnb", 2 * 1024 * 2, BF16, [128, 2, 1024])
    hqs = A.alloc("hqs", 2 * 1024 * 2, BF16, [128, 2, 1024])
    hqT = A.alloc("hqT", 2 * 22 * 128 * 2, BF16, [128, 2, 22, 128])
    x1f = A.alloc("x1f", 2 * D * 4, F32, [128, 2, D])
    qm = A.alloc("qm", 2 * 128 * 4, F32, [128, 2, 128])

    for c in range(8):
        for qn, buf, base in (("sync", wgq, 0), ("scalar", wuq, WQ_SEC)):
            h, v = sem.tick("rbw", 16)

            @on(qn)
            def _(e, c=c, buf=buf, base=base, h=h, first=(c == 0)):
                e.wait_ge(sem["cc"], V_CCW)
                if first:
                    for s, vv in P5_FENCE:
                        e.wait_ge(sem[s], vv)
                src = ccw_out.ap()[c, base:base + WQ_SEC] \
                    .rearrange("(p a b) -> p a b", p=128, b=NSH)
                e.dma_start(out=buf[:, :, c * NSH:(c + 1) * NSH],
                            in_=src).then_inc(h, 16)
    RBW_ALL = sem.val("rbw")

    for c in range(8):
        h, v = sem.tick("rbd", 16)

        @on("gpsimd")
        def _(e, c=c, h=h, first=(c == 0)):
            e.wait_ge(sem["cc"], V_CCW)
            if first:
                for s, vv in P5_FENCE:
                    e.wait_ge(sem[s], vv)
            src = ccw_out.ap()[c, 2 * WQ_SEC:3 * WQ_SEC] \
                .rearrange("(p a b) -> p a b", p=128, b=128)
            e.dma_start(out=wdq[:, :, c * 128:(c + 1) * 128],
                        in_=src).then_inc(h, 16)
    RBD_ALL = sem.val("rbd")
    dbg("wgq", wgq[:, 0, 0:512], [("rbw", RBW_ALL)])
    mark("rb")

    SECS = ((0, 1024), (1024, 1024), (2048, 768))
    s3_last = {}
    s4_done = {}
    out_ev = {}
    thq_v = []
    gq_v = []
    th_of_tile = {}
    xgl_v = {}
    gsec = 0

    def emit_s4(tt):
        tpar = tt % 2
        b0 = 0 if tt % 2 == 0 else 4
        h_s4, v_s4 = sem.tick("mm")

        @on("tensor")
        def _(e, tt=tt, tpar=tpar, b0=b0, v_th=th_of_tile[tt], h=h_s4,
              _gE=take_bank(b0), _gF=take_bank(b0 + 1), first=(tt == 0)):
            _w(e, _gE)
            _w(e, _gF)
            e.wait_ge(sem["thq"], v_th)
            if first:
                e.wait_ge(sem["rbd"], RBD_ALL)
            last = None
            for kc in range(22):
                st = hqT[:, tpar, kc, :]
                last = e.matmul(ps[:, b0, :], st, wdq[:, kc, 0:512],
                                start=(kc == 0), stop=(kc == 21))
                last = e.matmul(ps[:, b0 + 1, :], st, wdq[:, kc, 512:1024],
                                start=(kc == 0), stop=(kc == 21))
            last.then_inc(h, 1)
        s4_done[tt] = v_s4

        xsem = f"xrl{tt % 2}"
        h_xr, v_xr = sem.tick(xsem, 16)

        @on("sync")
        def _(e, tt=tt, h=h_xr, xsem=xsem):
            if tt >= 2:
                e.wait_ge(sem["outs"], 16 * (tt - 1))
                e.wait_ge(sem[xsem], 16 * (tt // 2))
            else:
                for s, vv in P5_FENCE:
                    e.wait_ge(sem[s], vv)
            e.wait_ge(sem[f"stx{tt % 2}"], 16 * (NT // 2))
            e.dma_start(out=x1f[:, tt % 2, :],
                        in_=x1_d[tt * 128:(tt + 1) * 128, :]).then_inc(h, 16)

        h_oe, v_oe = sem.tick("vec")

        @on("vector")
        def _(e, tt=tt, b0=b0, v_s4=s4_done[tt], v_xr=v_xr, xsem=xsem, h=h_oe):
            e.wait_ge(sem["mm"], v_s4)
            e.wait_ge(sem[xsem], v_xr)
            if tt >= 2:
                e.wait_ge(sem["outs"], 16 * (tt - 1))
            src = ps[:, b0:b0 + 2, :].rearrange("p a b -> p (a b)")
            e.tensor_tensor(out=x1f[:, tt % 2, :], in0=src,
                            in1=x1f[:, tt % 2, :], op=ALU.add).then_inc(h, 1)
        out_ev[tt] = v_oe
        guard_bank(b0, "vec", v_oe)
        guard_bank(b0 + 1, "vec", v_oe)

        h_os, v_os = sem.tick("outs", 16)

        @on("sync")
        def _(e, tt=tt, v_oe=v_oe, h=h_os):
            e.wait_ge(sem["vec"], v_oe)
            if tt >= 1:
                e.wait_ge(sem["outs"], 16 * tt)
            e.dma_start(out=out_t.ap()[tt * 128:(tt + 1) * 128, :],
                        in_=x1f[:, tt % 2, :]).then_inc(h, 16)

    for tt in range(NT):
        tpar = tt % 2
        h_xl, v_xl = sem.tick("xgl", 16)

        @on("gpsimd")
        def _(e, tt=tt, tpar=tpar, h=h_xl):
            e.wait_ge(sem["xqs"], xqs_st[tt])
            if tt == 0:
                for s, vv in P5_FENCE:
                    e.wait_ge(sem[s], vv)
            if tt >= 2:
                e.wait_ge(sem["mm"], s3_last[tt - 2])
                e.wait_ge(sem["xgl"], 16 * (tt - 1))
            e.dma_start(out=xqg[:, tpar],
                        in_=xq_d.ap()[:, :, tt * 128:(tt + 1) * 128]).then_inc(h, 16)
        xgl_v[tt] = v_xl

        for si, (s0, Ls) in enumerate(SECS):
            sp = gsec % 2
            bg = 0 if sp == 0 else 4
            bu = 2 if sp == 0 else 6
            c1 = Ls - 512
            h_mm, v_mm = sem.tick("mm")

            @on("tensor")
            def _(e, tt=tt, tpar=tpar, s0=s0, Ls=Ls, c1=c1, bg=bg, bu=bu,
                  v_xl=v_xl, h=h_mm,
                  first=(tt == 0 and si == 0), fw=(si == 0),
                  _gA=take_bank(bg), _gB=take_bank(bg + 1),
                  _gC=take_bank(bu), _gD=take_bank(bu + 1)):
                for gg in (_gA, _gB, _gC, _gD):
                    _w(e, gg)
                if first:
                    e.wait_ge(sem["rbw"], RBW_ALL)
                if fw:
                    e.wait_ge(sem["xgl"], v_xl)
                last = None
                for k in range(8):
                    st = xqg[:, tpar, k, :]
                    last = e.matmul(ps[:, bg, :], st, wgq[:, k, s0:s0 + 512],
                                    start=(k == 0), stop=(k == 7))
                    last = e.matmul(ps[:, bg + 1, 0:c1], st,
                                    wgq[:, k, s0 + 512:s0 + Ls],
                                    start=(k == 0), stop=(k == 7))
                    last = e.matmul(ps[:, bu, :], st, wuq[:, k, s0:s0 + 512],
                                    start=(k == 0), stop=(k == 7))
                    last = e.matmul(ps[:, bu + 1, 0:c1], st,
                                    wuq[:, k, s0 + 512:s0 + Ls],
                                    start=(k == 0), stop=(k == 7))
                last.then_inc(h, 1)

            h_sg, v_sg = sem.tick("act")

            @on("scalar")
            def _(e, Ls=Ls, bg=bg, v_mm=v_mm, h=h_sg,
                  _gh=take_buf("hsil")):
                e.wait_ge(sem["mm"], v_mm)
                _w(e, _gh)
                gsrc = ps[:, bg:bg + 2, :].rearrange("p a b -> p (a b)")[:, 0:Ls]
                e.activation(out=hsil[:, 0:Ls], in_=gsrc,
                             func=AF.Silu, bias=ZB()).then_inc(h, 1)
            guard_bank(bg, "act", v_sg)
            guard_bank(bg + 1, "act", v_sg)

            h_hv, v_hv = sem.tick("vec")

            @on("vector")
            def _(e, Ls=Ls, sp=sp, bu=bu, v_sg=v_sg, h=h_hv,
                  vGp=(gq_v[gsec - 2] if gsec >= 2 else None)):
                e.wait_ge(sem["act"], v_sg)
                if vGp is not None:
                    e.wait_ge(sem["gq"], vGp)
                usrc = ps[:, bu:bu + 2, :].rearrange("p a b -> p (a b)")[:, 0:Ls]
                e.tensor_tensor(out=hqh[:, sp, 0:Ls], in0=usrc,
                                in1=hsil[:, 0:Ls], op=ALU.mult)
                e.drain()
                NB = Ls // 16
                hh = hqh[:, sp, 0:Ls]
                bm = qm[:, sp, 0:NB]
                rb = qm[:, sp, 64:64 + NB]
                e.tensor_reduce(out=bm, in_=hh.rearrange("p (b k) -> p b k", k=16),
                                axis=AX.X, op=ALU.max, apply_absolute_value=True)
                e.drain()
                e.tensor_scalar(out=rb, in0=bm, scalar1=1e-30, scalar2=None,
                                op0=ALU.max)
                e.drain()
                e.reciprocal(out=rb, in_=rb)
                e.drain()
                e.tensor_scalar(out=bm, in0=bm, scalar1=1.0 / 12.0,
                                scalar2=None, op0=ALU.mult).then_inc(h, 1)
            guard_bank(bu, "vec", v_hv)
            guard_bank(bu + 1, "vec", v_hv)
            guard_buf("hsil", "vec", v_hv)

            h_sn, v_sn = sem.tick("act")

            @on("scalar")
            def _(e, Ls=Ls, sp=sp, v_hv=v_hv, h=h_sn,
                  vGp=(gq_v[gsec - 2] if gsec >= 2 else None)):
                e.wait_ge(sem["vec"], v_hv)
                if vGp is not None:
                    e.wait_ge(sem["gq"], vGp)
                hh = hqh[:, sp, 0:Ls]
                e.activation(out=sgnb[:, sp, 0:Ls], in_=hh, func=AF.Sign,
                             bias=ZB())
                e.drain()
                e.activation(out=hh, in_=hh, func=AF.Abs,
                             bias=ZB()).then_inc(h, 1)

            h_q3, v_q3 = sem.tick("vec")

            @on("vector")
            def _(e, Ls=Ls, sp=sp, v_sn=v_sn, h=h_q3,
                  vth=(thq_v[gsec - 2] if gsec >= 2 else None)):
                e.wait_ge(sem["act"], v_sn)
                if vth is not None:
                    e.wait_ge(sem["thq"], vth)
                NB = Ls // 16
                a = hqh[:, sp, 0:Ls]
                rb = qm[:, sp, 64:64 + NB]
                e.tensor_tensor(out=a.rearrange("p (b k) -> p b k", k=16),
                                in0=a.rearrange("p (b k) -> p b k", k=16),
                                in1=rb.unsqueeze(2).broadcast_to([128, NB, 16]),
                                op=ALU.mult)
                e.drain()
                e._custom_dve(FP4_P1, out=qff[:, sp, 0:Ls], in0=a,
                              s0=2.5 / 6.0, s1=3.5 / 6.0, imm2=5.0 / 6.0)
                e.drain()
                e._custom_dve(FP4_P2, out=hqs[:, sp, 0:Ls], in0=a,
                              in1=qff[:, sp, 0:Ls],
                              s0=0.25 / 6.0, s1=0.75 / 6.0, imm2=1.25 / 6.0)
                e.drain()
                e.scalar_tensor_tensor(out=qff[:, sp, 0:Ls], in0=a,
                                       scalar=1.75 / 6.0, in1=hqs[:, sp, 0:Ls],
                                       op0=ALU.is_gt, op1=ALU.add).then_inc(h, 1)

            h_gq2, v_gq2 = sem.tick("gq")

            @on("gpsimd")
            def _(e, Ls=Ls, sp=sp, v_q3=v_q3, h=h_gq2,
                  vth=(thq_v[gsec - 2] if gsec >= 2 else None)):
                e.wait_ge(sem["vec"], v_q3)
                if vth is not None:
                    e.wait_ge(sem["thq"], vth)
                NB = Ls // 16
                bm = qm[:, sp, 0:NB]
                ss = hqh[:, sp, 0:Ls]
                e.tensor_tensor(out=ss.rearrange("p (b k) -> p b k", k=16),
                                in0=sgnb[:, sp, 0:Ls]
                                .rearrange("p (b k) -> p b k", k=16),
                                in1=bm.unsqueeze(2).broadcast_to([128, NB, 16]),
                                op=ALU.mult)
                e.drain()
                e.tensor_tensor(out=hqs[:, sp, 0:Ls], in0=qff[:, sp, 0:Ls],
                                in1=ss, op=ALU.mult).then_inc(h, 1)
            gq_v.append(v_gq2)

            h_th, v_th = sem.tick("thq", 32)

            @on("sync")
            def _(e, tt=tt, tpar=tpar, s0=s0, Ls=Ls, sp=sp, gsec=gsec,
                  v_gq2=v_gq2, h=h_th,
                  vs4=(s4_done.get(tt - 2) if tt >= 2 else None)):
                e.wait_ge(sem["gq"], v_gq2)
                if gsec >= 1:
                    e.wait_ge(sem["thq"], 32 * gsec)
                if vs4 is not None:
                    e.wait_ge(sem["mm"], vs4)
                kc0 = s0 // 128
                e.dma_start_transpose(
                    hqT[:, tpar, kc0:kc0 + 4, :],
                    hqs[:, sp, 0:512]).then_inc(h, 16)
                e.dma_start_transpose(
                    hqT[:, tpar, kc0 + 4:kc0 + Ls // 128, :],
                    hqs[:, sp, 512:Ls]).then_inc(h, 16)
            thq_v.append(v_th)
            gsec += 1
        th_of_tile[tt] = thq_v[-1]
        s3_last[tt] = v_mm
        if tt >= 1:
            emit_s4(tt - 1)
    emit_s4(NT - 1)

    mark("all")
    if stage is not None:
        qlens, semsnap, ndbg = MARKS[stage]
        for k in Q:
            Q[k] = Q[k][:qlens[k]]
        dbg_list2 = dbg_list[:ndbg]
    else:
        semsnap = None
        dbg_list2 = dbg_list

    dbg_ticks = []
    for t, ap, waits in dbg_list2:
        h, v = sem.tick("outs", 16)
        dbg_ticks.append((t, ap, waits, h))
    OUTS_ALL = sem.val("outs")

    @blk.sync
    def _(e):
        for f in Q["sync"]:
            f(e)
        if semsnap is None:
            e.wait_ge(sem["outs"], OUTS_ALL)
        else:
            for k, v in semsnap.items():
                if v:
                    e.wait_ge(sem[k], v)
            e.wait_ge(sem["outs"], semsnap["outs"] + 16 * len(dbg_ticks))

    @blk.scalar
    def _(e):
        for f in Q["scalar"]:
            f(e)
        for t, ap, waits, h in dbg_ticks:
            for s, v in waits:
                e.wait_ge(sem[s], v)
            e.dma_start(out=t.ap(), in_=ap).then_inc(h, 16)

    @blk.vector
    def _(e):
        for f in Q["vector"]:
            f(e)

    @blk.tensor
    def _(e):
        for f in Q["tensor"]:
            f(e)

    @blk.gpsimd
    def _(e):
        for f in Q["gpsimd"]:
            f(e)

    es.close()
    mybir.codegen_inst_isa_subclasses(nc)
    return nc


def _in_maps(inputs):
    u = np.asarray(inputs["L0_attn_u"], np.float32)
    win = np.ascontiguousarray(np.asarray(inputs["L0_attn_W_in"], np.float32))
    wout = np.ascontiguousarray(np.asarray(inputs["L0_attn_W_out"], np.float32))
    wg = np.asarray(inputs["L0_ffn_w_gate"], np.float32)
    wu = np.asarray(inputs["L0_ffn_w_up"], np.float32)
    wd = np.asarray(inputs["L0_ffn_w_down"], np.float32)
    wB = np.asarray(inputs["L0_attn_wB"], np.float32)
    wC = np.asarray(inputs["L0_attn_wC"], np.float32)
    Bb = np.asarray(inputs["L0_attn_B_bias"], np.float32)
    Cb = np.asarray(inputs["L0_attn_C_bias"], np.float32)
    dtb = float(np.asarray(inputs["L0_attn_dt_bias"]).reshape(-1)[0])
    alog = float(np.asarray(inputs["L0_attn_A_log"]).reshape(-1)[0])
    vbc = np.zeros((128, 512), np.float32)
    vbc[:, 0:128] = wB[None, :]
    vbc[:, 128:256] = wC[None, :]
    vbc[:, 256:384] = Bb[None, :]
    vbc[:, 384:512] = Cb[None, :]
    ident = np.eye(128, dtype=np.float32)
    maps = []
    for c in range(8):
        b, h = c // 2, c % 2
        cstv = np.zeros((128, NCST), np.float32)
        cstv[:, C_EPS] = EPS
        cstv[64, C_AB] = dtb
        cstv[65, C_AB] = alog
        cstv[:, C_M0] = 1.0 if h == 0 else 0.0
        cstv[:, C_M1] = 1.0 if h == 1 else 0.0
        cstv[:, C_Q175] = 1.75
        cstv[:, C_ZERO] = 0.0
        cstv[:, C_ONE] = 1.0
        cstv[:, C_HPI] = math.pi / 2
        maps.append({
            "u": np.ascontiguousarray(u[b]),
            "uh": np.ascontiguousarray(u[b, h * LH:(h + 1) * LH]),
            "win": win,
            "wout": wout,
            "wg_sh": np.ascontiguousarray(wg[c * NSH:(c + 1) * NSH]),
            "wu_sh": np.ascontiguousarray(wu[c * NSH:(c + 1) * NSH]),
            "wd_sh": np.ascontiguousarray(wd[c * 128:(c + 1) * 128]),
            "vbc": vbc,
            "cst": cstv,
            "ident": ident,
        })
    return maps


def kernel(**inputs):
    nc = build_nc()
    maps = _in_maps(inputs)
    res = run_bass_kernel_spmd(nc, maps, core_ids=list(range(8)))
    out = np.empty((4, L, D), np.float32)
    for c, r in enumerate(res.results):
        b, h = c // 2, c % 2
        out[b, h * LH:(h + 1) * LH, :] = r["out"]
    return out


# version 07
# speedup vs baseline: 1.0971x; 1.0971x over previous
import math
import numpy as np
import concourse.bass as bass
import concourse.mybir as mybir
from concourse.bass_utils import run_bass_kernel_spmd

F32 = mybir.dt.float32
BF16 = mybir.dt.bfloat16
AF = mybir.ActivationFunctionType
ALU = mybir.AluOpType
AX = mybir.AxisListType

D, E, DS, TH, ZD, L = 1024, 2048, 128, 64, 4419, 4096
LH, I_FFN = 2048, 2816
NT, NTF = 16, 32
NSH = I_FFN // 8
EPS = 1e-5

_TWO_PI = 2.0 * math.pi
_CW1 = float(np.float32(6.28125))
_CW2 = float(np.float32(_TWO_PI - _CW1))
_CW3 = float(_TWO_PI - _CW1 - float(np.float32(_TWO_PI - _CW1)))
_MAGIC = 12582912.0
_INV2PI = 1.0 / _TWO_PI

C_EPS, C_AB, C_M0, C_M1, C_Q175, C_ZERO, C_ONE, C_HPI = 0, 1, 2, 3, 4, 5, 6, 7
NCST = 8

OFF_DTAL = 0
OFF_SSQ = 96
OFF_MX = 160
OFF_MXF = 176
OFF_ALPHA = 240
OFF_UM = 304
OFF_DTU = 336
OFF_DA = 368
OFF_YTOK = 400
OFF_YOWN = 432
OFF_X1M = 448
OFF_RSTD1 = 464
OFF_TMP = 480
NSME = 512

import concourse.dve_ops as dve_ops
from concourse.dve_op import DveOp
from concourse.dve_spec import Spec, lower, Src0, Src1, C0, C1, C2
from concourse.dve_table_gen import dve_ver_for


def _register_fp4_ops():
    if any(op.name == "ANT_FP4_P1" for op in dve_ops.OPS):
        return
    c1, c2, c3 = (Src0 > C0), (Src0 > C1), (Src0 > C2)
    s_ = (c1 + c2) + (c3 + c3)
    spec1 = Spec(body=s_ + s_)
    spec2 = Spec(body=((Src0 > C0) + (Src0 > C1)) + ((Src0 > C2) + Src1))

    def ref1(in0, in1, s0, s1, imm2):
        a = np.asarray(in0, np.float64)
        return (2.0 * (a > s0) + 2.0 * (a > s1) + 4.0 * (a > imm2)).astype(np.float32)

    def ref2(in0, in1, s0, s1, imm2):
        a = np.asarray(in0, np.float64)
        b = np.asarray(in1, np.float64)
        return ((a > s0) + (a > s1) + (a > imm2) + b).astype(np.float32)

    ver = dve_ver_for("TRN2")
    for name, spec, ref in (("ANT_FP4_P1", spec1, ref1), ("ANT_FP4_P2", spec2, ref2)):
        compiled = lower(spec, ver=ver, perf_en=False, name=name)
        op = DveOp(name, spec, subdim=False,
                   uops_sha={ver: compiled.sha(ver)}, reference=ref)
        dve_ops.OPS.append(op)
        dve_ops.CUSTOM_DVE_SPECS[name] = spec
        dve_ops.CUSTOM_DVE_REFS[name] = ref
        dve_ops._SUB_OPCODE_FOR_NAME[name] = (
            dve_ops._CUSTOM_DVE_ROW_BASE + len(dve_ops.OPS) - 1)


_register_fp4_ops()
FP4_P1 = next(op for op in dve_ops.OPS if op.name == "ANT_FP4_P1")
FP4_P2 = next(op for op in dve_ops.OPS if op.name == "ANT_FP4_P2")


class Sems:
    def __init__(self, nc, es, names):
        self.h = {n: es.enter_context(nc.semaphore(n)) for n in names}
        self.v = {n: 0 for n in names}

    def tick(self, name, n=1):
        self.v[name] += n
        return self.h[name], self.v[name]

    def val(self, name):
        return self.v[name]

    def __getitem__(self, name):
        return self.h[name]


class Arena:
    def __init__(self, nc, es, nbytes):
        self.nwords = nbytes // 4
        self.t = es.enter_context(nc.sbuf_tensor("arena", [128, self.nwords], F32))
        self.free_list = [(0, self.nwords)]
        self.live = {}

    def alloc(self, name, nbytes, dtype=F32, shape=None):
        words = (nbytes + 255) // 256 * 64
        cands = [(sz, idx) for idx, (off, sz) in enumerate(self.free_list)
                 if sz >= words]
        cands.sort()
        for idx in [c[1] for c in cands]:
            off, sz = self.free_list[idx]
            if True:
                if sz == words:
                    self.free_list.pop(idx)
                else:
                    self.free_list[idx] = (off + words, sz - words)
                self.live[name] = (off, words)
                ap = self.t[:, off:off + words]
                if dtype != F32:
                    ap = ap.bitcast(dtype)
                if shape is not None:
                    n = 1
                    for s in shape[1:]:
                        n *= s
                    ap = ap[:, 0:n]
                    if len(shape) == 3:
                        ap = ap.rearrange("p (a b) -> p a b", a=shape[1])
                    elif len(shape) == 4:
                        ap = ap.rearrange("p (a b c) -> p a b c",
                                          a=shape[1], b=shape[2])
                return ap
        raise RuntimeError(f"arena OOM for {name}: need {words}w "
                           f"free={self.free_list} live={list(self.live)}")

    def free(self, name):
        off, words = self.live.pop(name)
        self.free_list.append((off, words))
        self.free_list.sort()
        merged = []
        for off, sz in self.free_list:
            if merged and merged[-1][0] + merged[-1][1] == off:
                merged[-1] = (merged[-1][0], merged[-1][1] + sz)
            else:
                merged.append([off, sz])
        self.free_list = [tuple(x) for x in merged]


def build_nc(debug=(), stage=None):
    nc = bass.Bass("TRN2", target_bir_lowering=False, debug=False,
                   detect_race_conditions=True)
    from contextlib import ExitStack
    es = ExitStack()

    u_in = nc.dram_tensor("u", [L, D], F32, kind="ExternalInput").ap()
    uh_in = nc.dram_tensor("uh", [LH, D], F32, kind="ExternalInput").ap()
    win = nc.dram_tensor("win", [D, ZD], F32, kind="ExternalInput").ap()
    wout = nc.dram_tensor("wout", [E, D], F32, kind="ExternalInput").ap()
    wg_sh = nc.dram_tensor("wg_sh", [NSH, D], F32, kind="ExternalInput").ap()
    wu_sh = nc.dram_tensor("wu_sh", [NSH, D], F32, kind="ExternalInput").ap()
    wd_sh = nc.dram_tensor("wd_sh", [128, I_FFN], F32, kind="ExternalInput").ap()
    vbc_in = nc.dram_tensor("vbc", [128, 512], F32, kind="ExternalInput").ap()
    cst_in = nc.dram_tensor("cst", [128, NCST], F32, kind="ExternalInput").ap()
    id_in = nc.dram_tensor("ident", [128, 128], F32, kind="ExternalInput").ap()
    out_t = nc.dram_tensor("out", [LH, D], F32, kind="ExternalOutput")

    rows_d = nc.dram_tensor("rows_d", [6, L + 64], F32)
    x1_d = nc.dram_tensor("x1_d", [LH, D], F32)
    xq_d = nc.dram_tensor("xq_d", [128, 8, LH], BF16)
    cca_in = nc.dram_tensor("cca_in", [1, 1], F32)
    cca_out = nc.dram_tensor("cca_out", [8, 1], F32)
    ccs_in = nc.dram_tensor("ccs_in", [1, LH], F32)
    ccs_out = nc.dram_tensor("ccs_out", [2, LH], F32)
    WQ_SEC = 8 * 128 * NSH
    ccw_in = nc.dram_tensor("ccw_in", [1, 3 * WQ_SEC], BF16)
    ccw_out = nc.dram_tensor("ccw_out", [8, 3 * WQ_SEC], BF16)

    sem = Sems(nc, es, ["ld", "tr", "cc", "mm", "vec", "act",
                        "gq", "outs", "uld0", "uld1", "wqld", "wld", "xwld",
                        "gwld", "wobld", "xrl0", "xrl1", "url0", "url1",
                        "tyg", "txq", "thq", "xqs", "xgl", "rbd",
                        "stw", "stc", "stm", "stx0", "stx1",
                        "rba", "rbm", "rbw"])

    def sb(name, shape, dt):
        return es.enter_context(nc.sbuf_tensor(name, shape, dt))

    cst = sb("cstt", [128, NCST], F32)
    vbc = sb("vbct", [128, 512], F32)
    ident = sb("identt", [128, 128], F32)
    onesb = sb("onesb", [128, 128], F32)
    asc = sb("asct", [128, 1], F32)
    ascal = sb("ascal", [8, 2], F32)
    smeta = sb("smeta", [128, NSME], F32)
    stat = sb("stat", [128, NTF, 4], F32)
    mx4 = sb("mx4", [128, NT, 4], F32)
    wbc = sb("wbc", [128, 8, 257], BF16)
    wsm = sb("wsm", [128, 8, 67], BF16)
    mxr = sb("mxr", [NT, 128], F32)

    A = Arena(nc, es, 194 * 1024)
    ascr = A.alloc("ascr", 2 * D * 2, BF16, [128, 2 * D])
    lnut = A.alloc("lnut", 2 * D * 2, BF16, [128, 2, D])
    wmv = A.alloc("wmv", 8 * 512 * 2, BF16, [128, 8, 512])
    xsil = A.alloc("xsil", 512 * 2, BF16, [128, 512])
    ps = es.enter_context(nc.psum_tensor("ps", [128, 8, 512], F32))
    blk = es.enter_context(nc.Block())

    Q = {e: [] for e in ("sync", "scalar", "vector", "tensor", "gpsimd")}
    MARKS = {}

    def mark(name):
        MARKS[name] = ({k: len(v) for k, v in Q.items()}, dict(sem.v),
                       len(dbg_list))

    def on(engine):
        def deco(f):
            Q[engine].append(f)
        return deco

    dbg_list = []

    def dbg(name, ap, waits):
        if name not in debug:
            return
        t = nc.dram_tensor("dbg_" + name, list(ap.shape), ap.dtype,
                           kind="ExternalOutput")
        dbg_list.append((t, ap, list(waits)))

    buf_guard = {}
    bank_guard = {}

    def take_buf(name):
        return buf_guard.get(name)

    def guard_buf(name, s, v):
        buf_guard[name] = (s, v)

    def take_bank(i):
        return bank_guard.pop(i, None)

    def guard_bank(i, s, v):
        bank_guard[i] = (s, v)

    def _w(e, g):
        if g is not None:
            e.wait_ge(sem[g[0]], g[1])

    def tr_tick():
        prev = sem.val("tr")
        h, v = sem.tick("tr", 16)
        return h, v, prev

    ZB = lambda p0=0, pn=128: cst[p0:pn, C_ZERO:C_ZERO + 1]

    for dst, src in ((cst[:], cst_in), (vbc[:], vbc_in), (ident[:], id_in)):
        h, _ = sem.tick("ld", 16)

        @on("sync")
        def _(e, d=dst, s=src, h=h):
            e.dma_start(out=d, in_=s).then_inc(h, 16)
    LD_CONST = sem.val("ld")

    h, V_ONES = sem.tick("vec")

    @on("vector")
    def _(e, h=h):
        e.memset(onesb[:], 1.0)
        e.memset(stat[0:1, 3, 0:1], 0.0).then_inc(h, 1)

    lnuH0 = A.alloc("lnuH0", 8 * LH * 2, BF16, [128, 8, LH])
    lnuH1 = A.alloc("lnuH1", 8 * LH * 2, BF16, [128, 8, LH])
    ubuf = A.alloc("ubuf", 2 * D * 4, F32, [128, 2, D])
    N_UB = 2

    def lnu_half(hh):
        return lnuH0 if hh == 0 else lnuH1

    def lnu_mov(d, sl):
        hh, off = sl // 4, (sl % 4) * 512
        return lnu_half(hh)[:, d, off:off + 512]

    def lnu_stat(d, i):
        hh, off = i // 16, (i % 16) * 128
        return lnu_half(hh)[:, d, off:off + 128]

    u_tr = []
    lnt_done = []
    for i in range(NTF):
        bslot, lslot = i % N_UB, i % 2
        usem = f"uld{bslot}"
        h_ld, v_ld = sem.tick(usem, 16)

        @on("sync")
        def _(e, i=i, bslot=bslot, h=h_ld, usem=usem):
            if i >= N_UB:
                e.wait_ge(sem["vec"], lnt_done[i - N_UB])
                e.wait_ge(sem[usem], 16 * (i // N_UB))
            else:
                e.wait_ge(sem["ld"], LD_CONST)
            e.dma_start(out=ubuf[:, bslot, :],
                        in_=u_in[i * 128:(i + 1) * 128, :]).then_inc(h, 16)

        h_a1, v_a1 = sem.tick("act")

        @on("scalar")
        def _(e, i=i, bslot=bslot, v_ld=v_ld, usem=usem, h=h_a1, _g0=take_buf("ascr")):
            e.wait_ge(sem[usem], v_ld)
            e.wait_ge(sem["ld"], LD_CONST)
            _w(e, _g0)
            e.activation(out=ascr[:, 0:D], in_=ubuf[:, bslot, :],
                         func=AF.Identity, scale=1.0 / D, bias=ZB(),
                         accum_out=stat[:, i, 0:1])
            e.activation(out=ascr[:, D:2 * D], in_=ubuf[:, bslot, :],
                         func=AF.Square, scale=1.0 / 32.0, bias=ZB(),
                         accum_out=stat[:, i, 1:2]).then_inc(h, 1)
        guard_buf("ascr", "act", v_a1)

        h_v1, v_v1 = sem.tick("vec")

        @on("vector")
        def _(e, i=i, v_a1=v_a1, h=h_v1):
            e.wait_ge(sem["act"], v_a1)
            e.tensor_tensor(out=stat[:, i, 2:3], in0=stat[:, i, 0:1],
                            in1=stat[:, i, 0:1], op=ALU.mult)
            e.drain()
            e.tensor_tensor(out=stat[:, i, 2:3], in0=stat[:, i, 1:2],
                            in1=stat[:, i, 2:3], op=ALU.subtract).then_inc(h, 1)

        h_a2, v_a2 = sem.tick("act")

        @on("scalar")
        def _(e, i=i, v_v1=v_v1, h=h_a2):
            e.wait_ge(sem["vec"], v_v1)
            e.activation(out=stat[:, i, 3:4], in_=stat[:, i, 2:3], func=AF.Ln,
                         bias=cst[:, C_EPS:C_EPS + 1])
            e.drain()
            e.activation(out=stat[:, i, 1:2], in_=stat[:, i, 3:4], func=AF.Exp,
                         scale=-0.5, bias=ZB()).then_inc(h, 1)

        h_v2, v_v2 = sem.tick("vec")

        @on("vector")
        def _(e, i=i, bslot=bslot, lslot=lslot, v_a2=v_a2, h=h_v2):
            e.wait_ge(sem["act"], v_a2)
            if i >= 2:
                e.wait_ge(sem["tr"], u_tr[i - 2])
            e.tensor_scalar(out=lnut[:, lslot, :], in0=ubuf[:, bslot, :],
                            scalar1=stat[:, i, 0:1], scalar2=stat[:, i, 1:2],
                            op0=ALU.subtract, op1=ALU.mult).then_inc(h, 1)
        lnt_done.append(v_v2)

        h_tr, v_tr, p_tr = tr_tick()

        @on("scalar")
        def _(e, i=i, lslot=lslot, v_v2=v_v2, h=h_tr, p=p_tr):
            e.wait_ge(sem["vec"], v_v2)
            if p:
                e.wait_ge(sem["tr"], p)
            hh, loc = i // 16, (i % 16) * 128
            e.dma_start_transpose(lnu_half(hh)[:, :, loc:loc + 128],
                                  lnut[:, lslot, :]).then_inc(h, 16)
        u_tr.append(v_tr)
    TR_LNU = sem.val("tr")
    mark("p1pre")
    P1_LAST_APPLY = lnt_done[-1]
    dbg("lnu", lnuH0[:, :, 0:256], [("tr", TR_LNU)])
    dbg("rstd", stat[:, :, 0:2], [("tr", TR_LNU)])

    wbc_done, wsm_done = [], []
    for d in range(8):
        h1, v1 = sem.tick("wld", 16)
        h2, v2 = sem.tick("wld", 16)
        h3, v3 = sem.tick("wld", 16)
        h4, v4 = sem.tick("wld", 16)
        h5, v5 = sem.tick("wld", 16)

        @on("gpsimd")
        def _(e, d=d, h1=h1, h2=h2, h3=h3, h4=h4):
            r = win[d * 128:(d + 1) * 128, :]
            e.dma_start(out=wbc[:, d, 0:256], in_=r[:, 4096:4352]).then_inc(h1, 16)
            e.dma_start(out=wbc[:, d, 256:257], in_=r[:, 4418:4419],
                        allow_slow_non_contiguous=True).then_inc(h5, 16)
            e.dma_start(out=wsm[:, d, 0:64], in_=r[:, 4354:4418]).then_inc(h2, 16)
            e.dma_start(out=wsm[:, d, 64:66], in_=r[:, 4352:4354]).then_inc(h3, 16)
            e.dma_start(out=wsm[:, d, 66:67], in_=r[:, 4418:4419],
                        allow_slow_non_contiguous=True).then_inc(h4, 16)
        wbc_done.append(v5)
        wsm_done.append(v4)
    WLD_ALL = sem.val("wld")

    wqs_w = A.alloc("wqs_w", 512 * 4)
    wqs_a = A.alloc("wqs_a", 512 * 4)
    wqs_q = A.alloc("wqs_q", 2 * 512 * 4)
    wqs_b = A.alloc("wqs_b", 3 * 128 * 4)
    wqs_s = A.alloc("wqs_s", 2 * 512 * 2, BF16)
    wqs_o = A.alloc("wqs_o", 512 * 2, BF16)
    wqs_t = A.alloc("wqs_t", 512 * 2, BF16)

    WTILES = []
    for kind, srct in (("g", wg_sh), ("u", wu_sh)):
        for r0, rows in ((0, 128), (128, 128), (256, 96)):
            for kh in range(2):
                WTILES.append((kind, srct, r0, rows, kh * 512, 512, D))
    for k0 in range(0, I_FFN, 512):
        WTILES.append(("d", wd_sh, 0, 128, k0, min(512, I_FFN - k0), I_FFN))

    wq_gq_done = []
    wq_tr = []
    wq_sto = []
    for ti, (kind, srct, r0, rows, k0, kn, Ktot) in enumerate(WTILES):
        NB = kn // 16
        h_lw, v_lw = sem.tick("wqld", 16)

        @on("sync")
        def _(e, srct=srct, r0=r0, rows=rows, k0=k0, kn=kn, ti=ti, h=h_lw):
            if ti >= 1:
                e.wait_ge(sem["gq"], wq_gq_done[ti - 1])
                e.wait_ge(sem["wqld"], 16 * ti)
            else:
                e.wait_ge(sem["ld"], LD_CONST)
            e.dma_start(out=wqs_w[0:rows, 0:kn],
                        in_=srct[r0:r0 + rows, k0:k0 + kn]).then_inc(h, 16)

        h_v1, v_v1 = sem.tick("vec")

        @on("vector")
        def _(e, rows=rows, kn=kn, NB=NB, v_lw=v_lw, h=h_v1):
            e.wait_ge(sem["wqld"], v_lw)
            e.tensor_reduce(out=wqs_b[0:rows, 0:NB],
                            in_=wqs_w[0:rows, 0:kn]
                            .rearrange("p (b k) -> p b k", k=16),
                            axis=AX.X, op=ALU.max,
                            apply_absolute_value=True)
            e.drain()
            bm = wqs_b[0:rows, 0:NB]
            sh = wqs_b[0:rows, 128:128 + NB]
            rs = wqs_b[0:rows, 256:256 + NB]
            e.tensor_scalar(out=sh, in0=bm, scalar1=1.0 / 6.0, scalar2=1e-12,
                            op0=ALU.mult, op1=ALU.max)
            e.drain()
            e.reciprocal(out=rs, in_=sh)
            e.drain()
            e.tensor_scalar(out=sh, in0=sh, scalar1=0.5,
                            scalar2=None, op0=ALU.mult).then_inc(h, 1)

        h_g2, v_g2 = sem.tick("vec")

        @on("vector")
        def _(e, rows=rows, kn=kn, NB=NB, h=h_g2):
            e.drain()
            rs = wqs_b[0:rows, 256:256 + NB].unsqueeze(2) \
                .broadcast_to([rows, NB, 16])
            e.tensor_tensor(out=wqs_a[0:rows, 0:kn]
                            .rearrange("p (b k) -> p b k", k=16),
                            in0=wqs_w[0:rows, 0:kn]
                            .rearrange("p (b k) -> p b k", k=16),
                            in1=rs, op=ALU.mult).then_inc(h, 1)

        h_a1, v_a1 = sem.tick("act")

        @on("scalar")
        def _(e, rows=rows, kn=kn, v_g2=v_g2, h=h_a1):
            e.wait_ge(sem["vec"], v_g2)
            a = wqs_a[0:rows, 0:kn]
            e.activation(out=wqs_s[0:rows, 0:kn], in_=a, func=AF.Sign, bias=ZB(0, rows))
            e.drain()
            e.activation(out=a, in_=a, func=AF.Abs, bias=ZB(0, rows)).then_inc(h, 1)

        h_g3, v_g3 = sem.tick("gq")

        @on("vector")
        def _(e, rows=rows, kn=kn, NB=NB, ti=ti, v_a1=v_a1, h=h_g3):
            e.wait_ge(sem["act"], v_a1)
            if ti >= 1:
                e.wait_ge(sem["tr"], wq_tr[ti - 1])
            a = wqs_a[0:rows, 0:kn]
            q2v = wqs_q[0:rows, 0:kn]
            lvb = wqs_q[0:rows, 512:512 + kn]
            e._custom_dve(FP4_P1, out=q2v, in0=a, s0=2.5, s1=3.5, imm2=5.0)
            e.drain()
            e._custom_dve(FP4_P2, out=lvb, in0=a, in1=q2v,
                          s0=0.25, s1=0.75, imm2=1.25)
            e.drain()
            e.scalar_tensor_tensor(out=q2v, in0=a, scalar=1.75, in1=lvb,
                                   op0=ALU.is_gt, op1=ALU.add)
            e.drain()
            sh = wqs_b[0:rows, 128:128 + NB].unsqueeze(2) \
                .broadcast_to([rows, NB, 16])
            ss = wqs_s[0:rows, 512:512 + kn]
            e.tensor_tensor(out=ss.rearrange("p (b k) -> p b k", k=16),
                            in0=wqs_s[0:rows, 0:kn]
                            .rearrange("p (b k) -> p b k", k=16),
                            in1=sh, op=ALU.mult)
            e.drain()
            e.tensor_tensor(out=wqs_o[0:rows, 0:kn], in0=q2v, in1=ss,
                            op=ALU.mult).then_inc(h, 1)
        wq_gq_done.append(v_g3)

        NKC = kn // 128
        h_tw, v_tw, p_tw = tr_tick()

        @on("scalar")
        def _(e, rows=rows, kn=kn, NKC=NKC, ti=ti, v_g3=v_g3, h=h_tw, p=p_tw):
            e.wait_ge(sem["gq"], v_g3)
            if p:
                e.wait_ge(sem["tr"], p)
            if ti >= 1:
                e.wait_ge(sem["stw"], 16 * ti)
            e.dma_start_transpose(
                wqs_t[:, 0:NKC * rows].rearrange("p (a b) -> p a b", a=NKC),
                wqs_o[0:rows, 0:kn]).then_inc(h, 16)
        wq_tr.append(v_tw)

        h_sw, v_sw = sem.tick("stw", 16)

        @on("sync")
        def _(e, kind=kind, r0=r0, rows=rows, k0=k0, Ktot=Ktot, NKC=NKC,
              v_tw=v_tw, h=h_sw):
            e.wait_ge(sem["tr"], v_tw)
            base = {"g": 0, "u": WQ_SEC, "d": 2 * WQ_SEC}[kind]
            shard_n = NSH if kind != "d" else 128
            kc0 = k0 // 128
            dstap = ccw_in.ap()[0, base:base + (Ktot // 128) * 128 * shard_n] \
                .rearrange("(p a b) -> p a b", p=128, b=shard_n)[
                    :, kc0:kc0 + NKC, r0:r0 + rows]
            e.dma_start(out=dstap,
                        in_=wqs_t[:, 0:NKC * rows]
                        .rearrange("p (a b) -> p a b", a=NKC)).then_inc(h, 16)
        wq_sto.append(v_sw)
    WQ_STO = wq_sto[-1]
    WQ_CHAIN = wq_gq_done[-1]
    WQ_TR_LAST = wq_tr[-1]
    mark("wq")

    z67 = A.alloc("z67", L * 4, F32, [128, L])
    for sl in range(8):
        h_mm, v_mm = sem.tick("mm")

        @on("tensor")
        def _(e, sl=sl, h=h_mm, _g1=take_bank(6)):
            _w(e, _g1)
            e.wait_ge(sem["wld"], WLD_ALL)
            for i4 in range(4):
                e.wait_ge(sem["tr"], u_tr[sl * 4 + i4])
            last = None
            for d in range(8):
                mv = lnu_mov(d, sl)
                e.matmul(ps[0:64, 6, :], wsm[:, d, 0:64], mv,
                         start=(d == 0), stop=(d == 7))
                last = e.matmul(ps[64:66, 6, :], wsm[:, d, 64:66], mv,
                                start=(d == 0), stop=(d == 7))
            last.then_inc(h, 1)

        h_ev, v_ev = sem.tick("vec")

        @on("vector")
        def _(e, sl=sl, v_mm=v_mm, h=h_ev):
            e.wait_ge(sem["mm"], v_mm)
            e.tensor_copy(out=z67[0:66, sl * 512:(sl + 1) * 512],
                          in_=ps[0:66, 6, :]).then_inc(h, 1)
        guard_bank(6, "vec", v_ev)
    Z67 = sem.val("vec")
    guard_bank(6, "vec", Z67)
    dbg("z67", z67[0:66, 0:512], [("vec", Z67)])
    mark("z67")

    h_sp, A_SP = sem.tick("act")

    @on("scalar")
    def _(e, h=h_sp):
        e.wait_ge(sem["vec"], Z67)
        r = z67[64:66, 0:L]
        e.activation(out=r, in_=r, func=AF.Exp, bias=cst[64:66, C_AB:C_AB + 1])
        e.drain()
        e.activation(out=r, in_=r, func=AF.Ln, bias=cst[64:66, C_ONE:C_ONE + 1],
                     accum_out=stat[64:66, 0, 3:4]).then_inc(h, 1)


    for i in range(NTF):
        bk = 4 + (i % 2)
        h_mm, v_mm = sem.tick("mm")

        @on("tensor")
        def _(e, i=i, bk=bk, h=h_mm, _g2=take_bank(bk)):
            _w(e, _g2)
            e.wait_ge(sem["act"], A_SP)
            e.transpose(ps[:, bk, 0:2], z67[64:66, i * 128:(i + 1) * 128],
                        ident[64:66, 64:66]).then_inc(h, 1)

        h_cp, v_cp = sem.tick("vec")

        @on("vector")
        def _(e, i=i, bk=bk, v_mm=v_mm, h=h_cp):
            e.wait_ge(sem["mm"], v_mm)
            e.tensor_copy(out=smeta[:, OFF_DTAL + 3 * i:OFF_DTAL + 3 * i + 2],
                          in_=ps[:, bk, 0:2]).then_inc(h, 1)
        guard_bank(bk, "vec", v_cp)
    DTAL = sem.val("vec")
    dbg("dtal", smeta[:, OFF_DTAL:OFF_DTAL + 3 * NTF], [("vec", DTAL)])

    h_s3, v_s3 = sem.tick("stc", 16)

    @on("sync")
    def _(e, h=h_s3):
        e.wait_ge(sem["act"], A_SP)
        e.dma_start(out=cca_in[0, 0:1], in_=stat[65:66, 0, 3:4]).then_inc(h, 16)

    h_cca, V_CCA = sem.tick("cc")
    h_glA, AGA = sem.tick("rba", 16)

    @on("gpsimd")
    def _(e, h=h_cca, hg=h_glA, v=v_s3):
        e.wait_ge(sem["stc"], v)
        e.collective_compute("AllGather", ALU.bypass,
                             replica_groups=[list(range(8))],
                             ins=[cca_in.ap().opt()],
                             outs=[cca_out.ap().opt()]).then_inc(h, 1)
        e.wait_ge(sem["cc"], V_CCA)
        e.dma_start(out=ascal[:, 0:1], in_=cca_out.ap()).then_inc(hg, 16)

    h_ma, v_ma = sem.tick("mm")

    @on("tensor")
    def _(e, h=h_ma, _g3=take_bank(4)):
        _w(e, _g3)
        e.wait_ge(sem["rba"], AGA)
        e.wait_ge(sem["vec"], V_ONES)
        e.matmul(ps[0:1, 4, 0:1], onesb[0:8, 0:1], ascal[:, 0:1],
                 start=True, stop=True).then_inc(h, 1)

    h_va, v_va = sem.tick("vec")

    @on("vector")
    def _(e, v=v_ma, h=h_va):
        e.wait_ge(sem["mm"], v)
        e.tensor_scalar(out=stat[0:1, 1, 3:4], in0=ps[0:1, 4, 0:1],
                        scalar1=-1.0 / (2.0 * 4 * L), scalar2=None, op0=ALU.mult).then_inc(h, 1)
    guard_bank(4, "vec", v_va)

    h_mb, v_mb = sem.tick("mm")

    @on("tensor")
    def _(e, h=h_mb, v=v_va, _g4=take_bank(4)):
        _w(e, _g4)
        e.wait_ge(sem["vec"], v)
        e.matmul(ps[:, 4, 0:1], onesb[0:1, :], stat[0:1, 1, 3:4],
                 start=True, stop=True).then_inc(h, 1)

    h_vb, ASC = sem.tick("vec")

    @on("vector")
    def _(e, h=h_vb, v=v_mb):
        e.wait_ge(sem["mm"], v)
        e.tensor_copy(out=asc[:, 0:1], in_=ps[:, 4, 0:1]).then_inc(h, 1)
    guard_bank(4, "vec", ASC)
    dbg("asc", asc[:, 0:1], [("vec", ASC)])

    xw_done = {}
    x_mm_of_zs = {}

    def emit_xw_loads(zs):
        for d in range(8):
            slot = (zs * 8 + d) % 8
            h, v = sem.tick("xwld", 16)

            @on("gpsimd")
            def _(e, zs=zs, d=d, slot=slot, h=h):
                if zs >= 1 and d == 0:
                    e.wait_ge(sem["mm"], x_mm_of_zs[zs - 1])
                    e.wait_ge(sem["xwld"], 128 * zs)
                e.dma_start(out=wmv[:, slot, :],
                            in_=win[d * 128:(d + 1) * 128,
                                    zs * 512:(zs + 1) * 512]).then_inc(h, 16)
            xw_done[(zs, d)] = (128 * (zs + 1), slot)

    emit_xw_loads(0)

    mark("p2")

    bc_tok = A.alloc("bc_tok", NTF * 256 * 4, F32, [128, NTF, 256])
    for i in range(NTF):
        bk = i % 2
        h_mm, v_mm = sem.tick("mm")

        @on("tensor")
        def _(e, i=i, bk=bk, h=h_mm, _g5=take_bank(bk)):
            _w(e, _g5)
            e.wait_ge(sem["wld"], WLD_ALL)
            e.wait_ge(sem["tr"], u_tr[i])
            last = None
            for d in range(8):
                last = e.matmul(ps[:, bk, 0:257], lnu_stat(d, i),
                                wbc[:, d, :], start=(d == 0), stop=(d == 7))
            last.then_inc(h, 1)

        h_sq, v_sq = sem.tick("act")

        @on("scalar")
        def _(e, i=i, bk=bk, v_mm=v_mm, h=h_sq, _g6=take_buf("ascr")):
            e.wait_ge(sem["mm"], v_mm)
            _w(e, _g6)
            e.activation(out=ascr[:, 0:128], in_=ps[:, bk, 0:128],
                         func=AF.Square, bias=ZB(),
                         accum_out=smeta[:, OFF_SSQ + i:OFF_SSQ + i + 1])
            e.activation(out=ascr[:, 128:256], in_=ps[:, bk, 128:256],
                         func=AF.Square, bias=ZB(),
                         accum_out=smeta[:, OFF_SSQ + NTF + i:
                                         OFF_SSQ + NTF + i + 1]).then_inc(h, 1)
        guard_buf("ascr", "act", v_sq)

        h_cp, v_cp = sem.tick("vec")

        @on("vector")
        def _(e, i=i, bk=bk, v_sq=v_sq, h=h_cp):
            e.wait_ge(sem["act"], v_sq)
            e.tensor_copy(out=bc_tok[:, i, :], in_=ps[:, bk, 0:256])
            e.tensor_copy(out=smeta[:, OFF_DTAL + 3 * i + 2:OFF_DTAL + 3 * i + 3],
                          in_=ps[:, bk, 256:257]).then_inc(h, 1)
        guard_bank(bk, "vec", v_cp)
    BC_RAW = sem.val("vec")

    lamv = smeta[:, OFF_DTAL:OFF_DTAL + 3 * NTF] \
        .rearrange("p (t r) -> p r t", r=3)[:, 2, :]
    h_lams, LAMS_A = sem.tick("act")

    @on("scalar")
    def _(e, h=h_lams):
        e.wait_ge(sem["vec"], BC_RAW)
        e.activation(out=lamv, in_=lamv, func=AF.Exp, scale=-1.0,
                     bias=ZB()).then_inc(h, 1)

    h_lamv, LAMV = sem.tick("vec")

    @on("vector")
    def _(e, h=h_lamv):
        e.wait_ge(sem["act"], LAMS_A)
        e.tensor_scalar(out=lamv, in0=lamv, scalar1=1.0, scalar2=None,
                        op0=ALU.add)
        e.drain()
        e.reciprocal(out=lamv, in_=lamv).then_inc(h, 1)
    SSQ = sem.val("act")

    h_rsq, RSQ = sem.tick("act")

    @on("scalar")
    def _(e, h=h_rsq):
        e.wait_ge(sem["act"], SSQ)
        r = smeta[:, OFF_SSQ:OFF_SSQ + 2 * NTF]
        e.activation(out=r, in_=r, func=AF.Ln, scale=1.0 / 128.0,
                     bias=cst[:, C_EPS:C_EPS + 1])
        e.drain()
        e.activation(out=r, in_=r, func=AF.Exp, scale=-0.5,
                     bias=ZB()).then_inc(h, 1)

    h_bn, BC_NORM = sem.tick("vec")

    @on("vector")
    def _(e, h=h_bn):
        e.wait_ge(sem["vec"], BC_RAW)
        e.wait_ge(sem["act"], RSQ)
        v3 = bc_tok[:].rearrange("p t (c k) -> p t c k", c=2)
        rsq = smeta[:, OFF_SSQ:OFF_SSQ + 2 * NTF] \
            .rearrange("p (c t) -> p t c", c=2) \
            .unsqueeze(3).broadcast_to([128, NTF, 2, 128])
        e.tensor_tensor(out=v3, in0=v3, in1=rsq, op=ALU.mult)
        e.drain()
        wv = vbc[:, 0:256].rearrange("p (c k) -> p c k", c=2) \
            .unsqueeze(1).broadcast_to([128, NTF, 2, 128])
        e.tensor_tensor(out=v3, in0=v3, in1=wv, op=ALU.mult)
        e.drain()
        bv = vbc[:, 256:512].rearrange("p (c k) -> p c k", c=2) \
            .unsqueeze(1).broadcast_to([128, NTF, 2, 128])
        e.tensor_tensor(out=v3, in0=v3, in1=bv, op=ALU.add).then_inc(h, 1)
    dbg("bc_norm", bc_tok[:, 0:2, :], [("vec", BC_NORM)])
    mark("p2b")

    h_sel, LNUH = sem.tick("vec")

    @on("vector")
    def _(e, h=h_sel):
        e.wait_ge(sem["vec"], BC_RAW)
        e.wait_ge(sem["vec"], Z67)
        e.wait_ge(sem["ld"], LD_CONST)
        e.tensor_scalar(out=lnuH0[:], in0=lnuH0[:],
                        scalar1=cst[:, C_M0:C_M0 + 1], scalar2=None, op0=ALU.mult)
        e.drain()
        e.scalar_tensor_tensor(out=lnuH0[:], in0=lnuH1[:],
                               scalar=cst[:, C_M1:C_M1 + 1], in1=lnuH0[:],
                               op0=ALU.mult, op1=ALU.add).then_inc(h, 1)
    A.free("lnuH1")

    cum = A.alloc("cum", L * 4, F32, [128, L])
    for sl in range(8):
        h_mm, v_mm = sem.tick("mm")

        @on("tensor")
        def _(e, sl=sl, h=h_mm, _g7=take_bank(5)):
            _w(e, _g7)
            e.wait_ge(sem["act"], A_SP)
            e.wait_ge(sem["vec"], V_ONES)
            e.matmul(ps[0:64, 5, :], onesb[64:65, 0:64],
                     z67[64:65, sl * 512:(sl + 1) * 512],
                     start=True, stop=True).then_inc(h, 1)

        h_da, v_da = sem.tick("vec")

        @on("vector")
        def _(e, sl=sl, v_mm=v_mm, h=h_da):
            e.wait_ge(sem["mm"], v_mm)
            r = z67[0:64, sl * 512:(sl + 1) * 512]
            e.tensor_tensor(out=r, in0=r, in1=ps[0:64, 5, :],
                            op=ALU.mult).then_inc(h, 1)
        guard_bank(5, "vec", v_da)
    DANGLE = sem.val("vec")

    h_cum, CUM = sem.tick("vec")

    @on("vector")
    def _(e, h=h_cum):
        e.wait_ge(sem["vec"], DANGLE)
        e.tensor_tensor_scan(out=cum[0:64, :],
                             data0=onesb[0:64, 0:1].broadcast_to([64, L]),
                             data1=z67[0:64, 0:L], initial=0.0,
                             op0=ALU.mult, op1=ALU.add).then_inc(h, 1)
    dbg("cumang", cum[0:64, 0:512], [("vec", CUM)])

    A.free("z67") if False else None
    angt = A.alloc("angt", NTF * TH * 4, F32, [128, NTF * TH])
    for i in range(NTF):
        bk = 4 + (i % 2)
        h_mm, v_mm = sem.tick("mm")

        @on("tensor")
        def _(e, i=i, bk=bk, h=h_mm, _g8=take_bank(bk)):
            _w(e, _g8)
            e.wait_ge(sem["vec"], CUM)
            e.transpose(ps[:, bk, 0:64], cum[0:64, i * 128:(i + 1) * 128],
                        ident[0:64, 0:64]).then_inc(h, 1)

        h_cp, v_cp = sem.tick("vec")

        @on("vector")
        def _(e, i=i, bk=bk, v_mm=v_mm, h=h_cp):
            e.wait_ge(sem["mm"], v_mm)
            e.tensor_copy(out=angt[:, i * TH:(i + 1) * TH],
                          in_=ps[:, bk, 0:64]).then_inc(h, 1)
        guard_bank(bk, "vec", v_cp)
    ANGT = sem.val("vec")

    trig = A.alloc("trig", NTF * 2 * TH * 4, F32, [128, NTF, 2, TH])
    tsc = A.alloc("tsc", 2 * LH * 4, F32, [128, 2 * LH])

    h_trv, TRV = sem.tick("vec")

    @on("vector")
    def _(e, h=h_trv):
        e.wait_ge(sem["vec"], ANGT)
        a = angt[:]
        k = tsc[:, 0:LH]
        r = tsc[:, LH:2 * LH]
        e.tensor_scalar(out=k, in0=a, scalar1=_INV2PI, scalar2=_MAGIC,
                        op0=ALU.mult, op1=ALU.add)
        e.drain()
        e.tensor_scalar(out=k, in0=k, scalar1=_MAGIC, scalar2=None, op0=ALU.subtract)
        e.drain()
        e.cody_waite_cascade(r, a, k, _CW1, _CW2, _CW3)
        e.drain()
        e.sem_inc(h, 1)

    h_tra1, TRA1 = sem.tick("act")

    @on("scalar")
    def _(e, h=h_tra1):
        e.wait_ge(sem["vec"], TRV)
        r = tsc[:, LH:2 * LH].rearrange("p (t k) -> p t k", k=TH)
        e.activation(out=trig[:, :, 1, :], in_=r, func=AF.Sin,
                     bias=ZB()).then_inc(h, 1)

    h_trv2, TRV2 = sem.tick("vec")

    @on("vector")
    def _(e, h=h_trv2):
        e.wait_ge(sem["act"], TRA1)
        a = angt[:]
        k = tsc[:, 0:LH]
        r = tsc[:, LH:2 * LH]
        e.tensor_scalar(out=k, in0=a, scalar1=_INV2PI, scalar2=_MAGIC + 0.25,
                        op0=ALU.mult, op1=ALU.add)
        e.drain()
        e.tensor_scalar(out=k, in0=k, scalar1=_MAGIC, scalar2=None, op0=ALU.subtract)
        e.drain()
        e.cody_waite_cascade(r, a, k, _CW1, _CW2, _CW3)
        e.drain()
        e.sem_inc(h, 1)

    h_tra2, TRIG = sem.tick("act")

    @on("scalar")
    def _(e, h=h_tra2):
        e.wait_ge(sem["vec"], TRV2)
        r = tsc[:, LH:2 * LH].rearrange("p (t k) -> p t k", k=TH)
        e.activation(out=trig[:, :, 0, :], in_=r, func=AF.Sin,
                     bias=cst[:, C_HPI:C_HPI + 1]).then_inc(h, 1)
    dbg("trig", trig[:, 0:2, :, :], [("act", TRIG)])
    mark("p2c")

    for zs in range(4):
        if zs >= 1:
            emit_xw_loads(zs)
        for i in range(NT):
            bk = i % 2
            h_mm, v_mm = sem.tick("mm")

            @on("tensor")
            def _(e, zs=zs, i=i, bk=bk, h=h_mm, _g9=take_bank(bk)):
                _w(e, _g9)
                e.wait_ge(sem["vec"], LNUH)
                last = None
                e.wait_ge(sem["xwld"], xw_done[(zs, 0)][0])
                for d in range(8):
                    g = xw_done[(zs, d)]
                    last = e.matmul(ps[:, bk, :],
                                    lnuH0[:, d, i * 128:(i + 1) * 128],
                                    wmv[:, g[1], :], start=(d == 0),
                                    stop=(d == 7))
                last.then_inc(h, 1)
            x_mm_of_zs[zs] = v_mm

            h_sx, v_sx = sem.tick("act")

            @on("scalar")
            def _(e, zs=zs, i=i, bk=bk, v_mm=v_mm, h=h_sx, _g10=take_buf("xsil")):
                e.wait_ge(sem["mm"], v_mm)
                _w(e, _g10)
                e.activation(out=xsil[:], in_=ps[:, bk, :], func=AF.Silu,
                             bias=ZB(),
                             accum_out=mx4[:, i, zs:zs + 1]).then_inc(h, 1)
            guard_buf("xsil", "act", v_sx)
            guard_bank(bk, "act", v_sx)
    MX4 = sem.val("act")

    h_mx, MXTOK = sem.tick("vec")

    @on("vector")
    def _(e, h=h_mx):
        e.wait_ge(sem["act"], MX4)
        r = smeta[:, OFF_MX:OFF_MX + NT]
        e.tensor_reduce(out=r, in_=mx4[:], axis=AX.X, op=ALU.add)
        e.drain()
        e.tensor_scalar(out=r, in0=r, scalar1=1.0 / E, scalar2=None, op0=ALU.mult).then_inc(h, 1)

    h_mxt, v_mxt = sem.tick("mm")

    @on("tensor")
    def _(e, h=h_mxt, _g11=take_bank(4)):
        _w(e, _g11)
        e.wait_ge(sem["vec"], MXTOK)
        e.transpose(ps[0:NT, 4, 0:128], smeta[:, OFF_MX:OFF_MX + NT],
                    ident[:]).then_inc(h, 1)

    h_mxc, v_mxc = sem.tick("vec")

    @on("vector")
    def _(e, v=v_mxt, h=h_mxc):
        e.wait_ge(sem["mm"], v)
        e.tensor_copy(out=mxr[:], in_=ps[0:NT, 4, 0:128]).then_inc(h, 1)
    guard_bank(4, "vec", v_mxc)

    h_p1, v_p1 = sem.tick("stm", 16)

    @on("sync")
    def _(e, h=h_p1, v=v_mxc):
        e.wait_ge(sem["vec"], v)
        e.dma_start(out=ccs_in[0, 0:LH], in_=mxr[:]).then_inc(h, 16)

    h_ccw, V_CCW = sem.tick("cc")
    h_ccm, V_CCM = sem.tick("cc")

    @on("gpsimd")
    def _(e, hw=h_ccw, hm=h_ccm, vw=WQ_STO, vm=v_p1):
        e.wait_ge(sem["stw"], vw)
        e.collective_compute("AllGather", ALU.bypass,
                             replica_groups=[list(range(8))],
                             ins=[ccw_in.ap().opt()],
                             outs=[ccw_out.ap().opt()]).then_inc(hw, 1)
        e.wait_ge(sem["stm"], vm)
        e.collective_compute("AllGather", ALU.bypass,
                             replica_groups=[[0, 1], [2, 3], [4, 5], [6, 7]],
                             ins=[ccs_in.ap().opt()],
                             outs=[ccs_out.ap().opt()]).then_inc(hm, 1)

    A.free("wqs_w"); A.free("wqs_a"); A.free("wqs_q"); A.free("wqs_b")
    A.free("wqs_s"); A.free("wqs_o"); A.free("wqs_t")
    A.free("ubuf")
    mxsb = A.alloc("mxsb", (L + 128) * 4, F32, [128, L + 128])
    MXSB_FENCE = [("gq", WQ_CHAIN), ("tr", WQ_TR_LAST), ("stw", WQ_STO),
                  ("vec", P1_LAST_APPLY)]
    h_mz, MXZ = sem.tick("vec")

    @on("vector")
    def _(e, h=h_mz):
        for s, vv in MXSB_FENCE:
            e.wait_ge(sem[s], vv)
        e.memset(mxsb[0:1, 0:64], 0.0).then_inc(h, 1)

    h_m1, v_m1 = sem.tick("rbm", 16)
    h_m2, v_m2 = sem.tick("rbm", 16)

    @on("sync")
    def _(e, h1=h_m1, h2=h_m2, vz=MXZ):
        e.wait_ge(sem["cc"], V_CCM)
        e.wait_ge(sem["vec"], vz)
        for s, vv in MXSB_FENCE:
            e.wait_ge(sem[s], vv)
        e.dma_start(out=mxsb[0:1, 64:64 + LH],
                    in_=ccs_out.ap()[0:1, 0:LH]).then_inc(h1, 16)
        e.dma_start(out=mxsb[0:1, 64 + LH:64 + L],
                    in_=ccs_out.ap()[1:2, 0:LH]).then_inc(h2, 16)

    for i in range(NTF):
        bk = 4 + (i % 2)
        h_mm, v_mm = sem.tick("mm")

        @on("tensor")
        def _(e, i=i, bk=bk, h=h_mm, _g12=take_bank(bk)):
            _w(e, _g12)
            e.wait_ge(sem["rbm"], v_m2)
            e.transpose(ps[:, bk, 0:1],
                        mxsb[0:1, 64 + i * 128:64 + (i + 1) * 128],
                        ident[0:1, 0:1])
            e.transpose(ps[:, bk, 1:2],
                        mxsb[0:1, 63 + i * 128:63 + (i + 1) * 128],
                        ident[0:1, 0:1]).then_inc(h, 1)

        h_cp, v_cp = sem.tick("vec")

        @on("vector")
        def _(e, i=i, bk=bk, v_mm=v_mm, h=h_cp):
            e.wait_ge(sem["mm"], v_mm)
            e.tensor_copy(out=smeta[:, OFF_MXF + 2 * i:OFF_MXF + 2 * i + 2],
                          in_=ps[:, bk, 0:2]).then_inc(h, 1)
        guard_bank(bk, "vec", v_cp)
    MXTOKF = sem.val("vec")
    A.free("mxsb")
    mark("p3")

    h_um1, UM1 = sem.tick("vec")

    @on("vector")
    def _(e, h=h_um1):
        e.wait_ge(sem["vec"], MXTOKF)
        e.wait_ge(sem["vec"], DTAL)
        e.wait_ge(sem["vec"], LAMV)
        r3 = smeta[:, OFF_DTAL:OFF_DTAL + 3 * NTF] \
            .rearrange("p (t r) -> p r t", r=3)
        t1 = smeta[:, OFF_ALPHA:OFF_ALPHA + NTF]
        e.tensor_tensor(out=t1, in0=r3[:, 0, :], in1=r3[:, 1, :],
                        op=ALU.mult).then_inc(h, 1)

    h_um2, UM2 = sem.tick("act")

    @on("scalar")
    def _(e, h=h_um2):
        e.wait_ge(sem["vec"], UM1)
        t1 = smeta[:, OFF_ALPHA:OFF_ALPHA + NTF]
        e.activation(out=t1, in_=t1, func=AF.Exp, scale=-1.0,
                     bias=ZB()).then_inc(h, 1)

    h_um3, UM3 = sem.tick("vec")

    @on("vector")
    def _(e, h=h_um3):
        e.wait_ge(sem["act"], UM2)
        r3 = smeta[:, OFF_DTAL:OFF_DTAL + 3 * NTF] \
            .rearrange("p (t r) -> p r t", r=3)
        dt, lam = r3[:, 0, :], r3[:, 2, :]
        r2 = smeta[:, OFF_MXF:OFF_MXF + 2 * NTF] \
            .rearrange("p (t r) -> p r t", r=2)
        mx, mxp = r2[:, 0, :], r2[:, 1, :]
        alpha = smeta[:, OFF_ALPHA:OFF_ALPHA + NTF]
        um = smeta[:, OFF_UM:OFF_UM + NTF]
        dtu = smeta[:, OFF_DTU:OFF_DTU + NTF]
        tmp = smeta[:, OFF_TMP:OFF_TMP + NTF]
        e.tensor_tensor(out=um, in0=alpha, in1=mxp, op=ALU.mult)
        e.drain()
        e.scalar_tensor_tensor(out=tmp, in0=lam, scalar=-1.0, in1=um,
                               op0=ALU.mult, op1=ALU.mult)
        e.drain()
        e.tensor_tensor(out=um, in0=um, in1=tmp, op=ALU.add)
        e.drain()
        e.tensor_tensor(out=tmp, in0=lam, in1=mx, op=ALU.mult)
        e.drain()
        e.tensor_tensor(out=um, in0=um, in1=tmp, op=ALU.add)
        e.drain()
        e.tensor_tensor(out=dtu, in0=dt, in1=um, op=ALU.mult).then_inc(h, 1)
    dbg("umean", smeta[:, OFF_UM:OFF_UM + NTF], [("vec", UM3)])
    mark("p4a")

    h_rp, ROPE = sem.tick("vec")

    @on("vector")
    def _(e, h=h_rp):
        e.wait_ge(sem["vec"], BC_NORM)
        e.wait_ge(sem["act"], TRIG)
        e.wait_ge(sem["vec"], UM3)
        cos = trig[:, :, 0, :]
        sin = trig[:, :, 1, :]
        m1 = tsc[:, 0:LH].rearrange("p (t k) -> p t k", k=TH)
        m2 = tsc[:, LH:2 * LH].rearrange("p (t k) -> p t k", k=TH)
        for c in range(2):
            lo = bc_tok[:, :, c * 128:c * 128 + TH]
            hi = bc_tok[:, :, c * 128 + TH:c * 128 + 128]
            e.tensor_tensor(out=m1, in0=lo, in1=cos, op=ALU.mult)
            e.tensor_tensor(out=m2, in0=hi, in1=sin, op=ALU.mult)
            e.drain()
            e.tensor_tensor(out=m2, in0=m1, in1=m2, op=ALU.subtract)
            e.drain()
            e.tensor_tensor(out=m1, in0=lo, in1=sin, op=ALU.mult)
            e.drain()
            e.tensor_copy(out=lo, in_=m2)
            e.drain()
            e.tensor_tensor(out=m2, in0=hi, in1=cos, op=ALU.mult)
            e.drain()
            e.tensor_tensor(out=hi, in0=m1, in1=m2, op=ALU.add)
            e.drain()
        bpart = bc_tok[:].rearrange("p t (c k) -> p t c k", c=2)[:, :, 0, :]
        dtub = smeta[:, OFF_DTU:OFF_DTU + NTF].unsqueeze(2) \
            .broadcast_to([128, NTF, 128])
        e.tensor_tensor(out=bpart, in0=bpart, in1=dtub,
                        op=ALU.mult).then_inc(h, 1)
    dbg("rope", bc_tok[:, 0:2, :], [("vec", ROPE)])

    A.free("trig")
    A.free("angt")
    A.free("tsc")
    dbu = A.alloc("dbu", L * 4, F32, [128, L])
    ctm = A.alloc("ctm", L * 4, F32, [128, L])
    for i in range(NTF):
        bkB, bkC = 4 + (i % 2), 6 + (i % 2)
        h_mm, v_mm = sem.tick("mm")

        @on("tensor")
        def _(e, i=i, bkB=bkB, bkC=bkC, h=h_mm, _g13=take_bank(bkB), _g14=take_bank(bkC)):
            _w(e, _g13)
            _w(e, _g14)
            e.wait_ge(sem["vec"], ROPE)
            e.transpose(ps[:, bkB, 0:128], bc_tok[:, i, 0:128], ident[:])
            e.transpose(ps[:, bkC, 0:128], bc_tok[:, i, 128:256],
                        ident[:]).then_inc(h, 1)

        h_cp, v_cp = sem.tick("vec")

        @on("vector")
        def _(e, i=i, bkB=bkB, bkC=bkC, v_mm=v_mm, h=h_cp):
            e.wait_ge(sem["mm"], v_mm)
            e.tensor_copy(out=dbu[:, i * 128:(i + 1) * 128], in_=ps[:, bkB, 0:128])
            e.tensor_copy(out=ctm[:, i * 128:(i + 1) * 128],
                          in_=ps[:, bkC, 0:128]).then_inc(h, 1)
        guard_bank(bkB, "vec", v_cp)
        guard_bank(bkC, "vec", v_cp)
    TR_BC = sem.val("vec")

    A.free("bc_tok")
    dabc = A.alloc("dabc", L * 4, F32, [128, L])
    dab_acts = []
    for sl in range(8):
        bk = 4 + (sl % 2)
        h_mm, v_mm = sem.tick("mm")

        @on("tensor")
        def _(e, sl=sl, bk=bk, h=h_mm, _g15=take_bank(bk)):
            _w(e, _g15)
            e.wait_ge(sem["act"], A_SP)
            e.wait_ge(sem["vec"], ASC)
            e.matmul(ps[:, bk, :], onesb[64:65, :],
                     z67[64:65, sl * 512:(sl + 1) * 512],
                     start=True, stop=True).then_inc(h, 1)

        h_cp, v_cp = sem.tick("act")

        @on("scalar")
        def _(e, sl=sl, bk=bk, v_mm=v_mm, h=h_cp):
            e.wait_ge(sem["mm"], v_mm)
            e.wait_ge(sem["vec"], ROPE)
            e.activation(out=dabc[:, sl * 512:(sl + 1) * 512],
                         in_=ps[:, bk, :], func=AF.Exp, scale=asc[:, 0:1],
                         bias=ZB()).then_inc(h, 1)
        guard_bank(bk, "act", v_cp)
        dab_acts.append(v_cp)
    DABC_A = dab_acts[-1]
    A.free("z67")

    states = A.alloc("states", L * 4, F32, [128, L])
    h_scan, SCAN = sem.tick("vec")

    @on("vector")
    def _(e, h=h_scan):
        e.wait_ge(sem["act"], DABC_A)
        e.wait_ge(sem["vec"], TR_BC)
        e.tensor_tensor_scan(out=states[:], data0=dabc[:], data1=dbu[:],
                             initial=0.0, op0=ALU.mult, op1=ALU.add)
        e.drain()
        e.tensor_tensor(out=ctm[:], in0=states[:], in1=ctm[:],
                        op=ALU.mult).then_inc(h, 1)
    dbg("states", states[:, 0:512], [("vec", SCAN)])

    for sl in range(8):
        bk = 4 + (sl % 2)
        h_mm, v_mm = sem.tick("mm")

        @on("tensor")
        def _(e, sl=sl, bk=bk, h=h_mm, _g16=take_bank(bk)):
            _w(e, _g16)
            e.wait_ge(sem["vec"], SCAN)
            e.matmul(ps[0:1, bk, :], onesb[:, 0:1],
                     ctm[:, sl * 512:(sl + 1) * 512],
                     start=True, stop=True).then_inc(h, 1)

        h_cp, v_cp = sem.tick("vec")

        @on("vector")
        def _(e, sl=sl, bk=bk, v_mm=v_mm, h=h_cp):
            e.wait_ge(sem["mm"], v_mm)
            e.tensor_copy(out=dabc[0:1, sl * 512:(sl + 1) * 512],
                          in_=ps[0:1, bk, :]).then_inc(h, 1)
        guard_bank(bk, "vec", v_cp)
    YROW = sem.val("vec")

    for i in range(NTF):
        bk = 4 + (i % 2)
        h_mm, v_mm = sem.tick("mm")

        @on("tensor")
        def _(e, i=i, bk=bk, h=h_mm, _g17=take_bank(bk)):
            _w(e, _g17)
            e.wait_ge(sem["vec"], YROW)
            e.transpose(ps[:, bk, 0:1],
                        dabc[0:1, i * 128:(i + 1) * 128],
                        ident[0:1, 0:1]).then_inc(h, 1)

        h_cp, v_cp = sem.tick("vec")

        @on("vector")
        def _(e, i=i, bk=bk, v_mm=v_mm, h=h_cp):
            e.wait_ge(sem["mm"], v_mm)
            e.tensor_copy(out=smeta[:, OFF_YTOK + i:OFF_YTOK + i + 1],
                          in_=ps[:, bk, 0:1]).then_inc(h, 1)
        guard_bank(bk, "vec", v_cp)
    YTOKV = sem.val("vec")

    h_yo, YOWN = sem.tick("vec")

    @on("vector")
    def _(e, h=h_yo):
        e.wait_ge(sem["vec"], YTOKV)
        yf = smeta[:, OFF_YTOK:OFF_YTOK + NTF]
        yo = smeta[:, OFF_YOWN:OFF_YOWN + NT]
        e.tensor_scalar(out=yo, in0=yf[:, 0:NT], scalar1=cst[:, C_M0:C_M0 + 1],
                        scalar2=None, op0=ALU.mult)
        e.drain()
        e.scalar_tensor_tensor(out=yo, in0=yf[:, NT:NTF],
                               scalar=cst[:, C_M1:C_M1 + 1], in1=yo,
                               op0=ALU.mult, op1=ALU.add).then_inc(h, 1)
    dbg("yown", smeta[:, OFF_YOWN:OFF_YOWN + NT], [("vec", YOWN)])
    mark("p4")

    A.free("cum")
    A.free("dbu")
    A.free("ctm")
    A.free("dabc")
    A.free("states")
    A.free("wmv")
    A.free("xsil")
    wgt = A.alloc("wgt", 32 * 512 * 2, BF16, [128, 32, 512])
    wob = A.alloc("wob", 16 * D * 2, BF16, [128, 16, D])
    gtok = A.alloc("gtok", E * 2, BF16, [128, E])
    ygT = A.alloc("ygT", 2 * 16 * 128 * 2, BF16, [128, 2, 16, 128])
    x1t = A.alloc("x1t", 2 * D * 4, F32, [128, 2, D])
    urest = A.alloc("urest", 2 * D * 4, F32, [128, 2, D])
    qsc = A.alloc("qsc", (3 * D + 128) * 4, F32, [128, 3 * D + 128])
    xqt = A.alloc("xqt", 2 * D * 2, BF16, [128, 2, D])
    xqs = A.alloc("xqs", 2 * 8 * 128 * 2, BF16, [128, 2, 8, 128])
    qbf = A.alloc("qbf", 2 * D * 2, BF16, [128, 2 * D])

    WGT_FENCE = [("vec", LNUH), ("vec", CUM), ("vec", DANGLE),
                 ("gq", WQ_CHAIN), ("tr", WQ_TR_LAST), ("stw", WQ_STO),
                 ("vec", P1_LAST_APPLY), ("act", A_SP), ("vec", LAMV),
                 ("vec", YROW), ("vec", TR_BC), ("vec", YTOKV),
                 ("vec", ROPE), ("vec", SCAN), ("act", DABC_A)]
    wgt_done = []
    for zs in range(4):
        for d in range(8):
            h, v = sem.tick("gwld", 16)

            @on("gpsimd")
            def _(e, zs=zs, d=d, h=h, first=(zs == 0 and d == 0)):
                if first:
                    for s, vv in WGT_FENCE:
                        e.wait_ge(sem[s], vv)
                e.dma_start(out=wgt[:, zs * 8 + d, :],
                            in_=win[d * 128:(d + 1) * 128,
                                    E + zs * 512:E + (zs + 1) * 512]).then_inc(h, 16)
            wgt_done.append(v)
    WGT = wgt_done[-1]
    wob_done = []
    for ec in range(16):
        h, v = sem.tick("wobld", 16)

        @on("gpsimd")
        def _(e, ec=ec, h=h):
            if ec == 0:
                e.wait_ge(sem["gwld"], WGT)
            e.dma_start(out=wob[:, ec, :],
                        in_=wout[ec * 128:(ec + 1) * 128, :]).then_inc(h, 16)
        wob_done.append(v)
    WOB = wob_done[-1]

    s2_done = []
    x1_done = []
    xq_tr = []
    yg_tr = []
    xqs_st = []
    xq_gq = []
    xq_vq3 = []
    for i in range(NT):
        gsl = i % 2
        h_gm, v_gm = sem.tick("mm")

        @on("tensor")
        def _(e, i=i, h=h_gm, _g18=[take_bank(b) for b in range(4)]):
            for g in _g18:
                _w(e, g)
            e.wait_ge(sem["gwld"], WGT)
            last = None
            for zs in range(4):
                for d in range(8):
                    last = e.matmul(ps[:, zs, :],
                                    lnuH0[:, d, i * 128:(i + 1) * 128],
                                    wgt[:, zs * 8 + d, :],
                                    start=(d == 0), stop=(d == 7))
            last.then_inc(h, 1)

        h_gs, v_gs = sem.tick("act")

        @on("scalar")
        def _(e, i=i, v_gm=v_gm, h=h_gs):
            e.wait_ge(sem["mm"], v_gm)
            if i >= 1:
                e.wait_ge(sem["tyg"], 16 * i)
            last = None
            for zs in range(4):
                last = e.activation(out=gtok[:, zs * 512:(zs + 1) * 512],
                                    in_=ps[:, zs, :], func=AF.Silu, bias=ZB())
            last.then_inc(h, 1)
        for b4 in range(4):
            guard_bank(b4, "act", v_gs)

        h_yg, v_yg = sem.tick("vec")

        @on("vector")
        def _(e, i=i, v_gs=v_gs, h=h_yg):
            e.wait_ge(sem["act"], v_gs)
            e.wait_ge(sem["vec"], YOWN)
            g = gtok[:]
            e.tensor_scalar(out=g, in0=g,
                            scalar1=smeta[:, OFF_YOWN + i:OFF_YOWN + i + 1],
                            scalar2=None, op0=ALU.mult).then_inc(h, 1)

        h_ty, v_ty = sem.tick("tyg", 16)

        @on("sync")
        def _(e, i=i, gsl=gsl, v_yg=v_yg, h=h_ty):
            e.wait_ge(sem["vec"], v_yg)
            if i >= 1:
                e.wait_ge(sem["tyg"], 16 * i)
            if i >= 2:
                e.wait_ge(sem["mm"], s2_done[i - 2])
            e.dma_start_transpose(ygT[:, gsl, :, :],
                                  gtok[:]).then_inc(h, 16)
        yg_tr.append(v_ty)

        h_s2, v_s2b = sem.tick("mm")

        @on("tensor")
        def _(e, i=i, gsl=gsl, v_ty=v_ty, h=h_s2, _g19=take_bank(4), _g20=take_bank(5)):
            _w(e, _g19)
            _w(e, _g20)
            e.wait_ge(sem["tyg"], v_ty)
            e.wait_ge(sem["wobld"], WOB)
            last = None
            for ec in range(16):
                for half in range(2):
                    last = e.matmul(ps[:, 4 + half, :], ygT[:, gsl, ec, :],
                                    wob[:, ec, half * 512:(half + 1) * 512],
                                    start=(ec == 0), stop=(ec == 15))
            last.then_inc(h, 1)
        s2_done.append(v_s2b)

        usem5 = f"url{i % 2}"
        h_ur, v_ur = sem.tick(usem5, 16)

        @on("sync")
        def _(e, i=i, h=h_ur, usem5=usem5):
            if i >= 2:
                e.wait_ge(sem["vec"], x1_done[i - 2])
                e.wait_ge(sem[usem5], 16 * (i // 2))
            else:
                for s, vv in WGT_FENCE:
                    e.wait_ge(sem[s], vv)
            e.dma_start(out=urest[:, i % 2, :],
                        in_=uh_in[i * 128:(i + 1) * 128, :]).then_inc(h, 16)

        h_x1, v_x1 = sem.tick("vec")

        @on("vector")
        def _(e, i=i, v_s2b=v_s2b, v_ur=v_ur, usem5=usem5, h=h_x1):
            e.wait_ge(sem["mm"], v_s2b)
            e.wait_ge(sem[usem5], v_ur)
            if i >= 2:
                e.wait_ge(sem[f"stx{i % 2}"], 16 * (i // 2))
            last = None
            for half in range(2):
                last = e.tensor_tensor(
                    out=x1t[:, i % 2, half * 512:(half + 1) * 512],
                    in0=ps[:, 4 + half, :],
                    in1=urest[:, i % 2, half * 512:(half + 1) * 512],
                    op=ALU.add)
            last.then_inc(h, 1)
        x1_done.append(v_x1)
        guard_bank(4, "vec", v_x1)
        guard_bank(5, "vec", v_x1)

        ssem5 = f"stx{i % 2}"
        h_xs, v_xs = sem.tick(ssem5, 16)

        @on("sync")
        def _(e, i=i, v_x1=v_x1, h=h_xs, ssem5=ssem5):
            e.wait_ge(sem["vec"], v_x1)
            if i >= 2:
                e.wait_ge(sem[ssem5], 16 * (i // 2))
            e.dma_start(out=x1_d[i * 128:(i + 1) * 128, :],
                        in_=x1t[:, i % 2, :]).then_inc(h, 16)

        h_ls, v_ls = sem.tick("act")

        @on("scalar")
        def _(e, i=i, v_x1=v_x1, h=h_ls, _g21=take_buf("ascr")):
            e.wait_ge(sem["vec"], v_x1)
            _w(e, _g21)
            e.activation(out=ascr[:, 0:D], in_=x1t[:, i % 2, :],
                         func=AF.Identity, scale=-1.0 / D, bias=ZB(),
                         accum_out=smeta[:, OFF_X1M + i:OFF_X1M + i + 1])
            e.activation(out=ascr[:, D:2 * D], in_=x1t[:, i % 2, :],
                         func=AF.Square, scale=1.0 / 32.0, bias=ZB(),
                         accum_out=smeta[:, OFF_RSTD1 + i:
                                         OFF_RSTD1 + i + 1]).then_inc(h, 1)
        guard_buf("ascr", "act", v_ls)

        h_vv, v_vv = sem.tick("vec")

        @on("vector")
        def _(e, i=i, v_ls=v_ls, h=h_vv):
            e.wait_ge(sem["act"], v_ls)
            m = smeta[:, OFF_X1M + i:OFF_X1M + i + 1]
            ssq = smeta[:, OFF_RSTD1 + i:OFF_RSTD1 + i + 1]
            tmp = smeta[:, OFF_TMP:OFF_TMP + 1]
            e.tensor_scalar(out=tmp, in0=m, scalar1=m, scalar2=None,
                            op0=ALU.mult)
            e.drain()
            e.tensor_tensor(out=ssq, in0=ssq, in1=tmp,
                            op=ALU.subtract).then_inc(h, 1)

        h_rs, v_rs = sem.tick("act")

        @on("scalar")
        def _(e, i=i, v_vv=v_vv, h=h_rs,
              vGp=(xq_gq[i - 1] if i >= 1 else None)):
            e.wait_ge(sem["vec"], v_vv)
            if vGp is not None:
                e.wait_ge(sem["vec"], vGp)
            r = smeta[:, OFF_RSTD1 + i:OFF_RSTD1 + i + 1]
            e.activation(out=r, in_=r, func=AF.Ln, bias=cst[:, C_EPS:C_EPS + 1])
            e.drain()
            e.activation(out=r, in_=r, func=AF.Exp, scale=-0.5, bias=ZB())
            e.drain()
            e.activation(out=qsc[:, 2 * D:3 * D], in_=x1t[:, i % 2, :],
                         func=AF.Identity,
                         bias=smeta[:, OFF_X1M + i:OFF_X1M + i + 1])
            e.drain()
            e.activation(out=qbf[:, 0:D], in_=qsc[:, 2 * D:3 * D],
                         func=AF.Sign, bias=ZB()).then_inc(h, 1)

        h_q1, v_q1 = sem.tick("vec")

        @on("vector")
        def _(e, i=i, v_rs=v_rs, h=h_q1):
            e.wait_ge(sem["act"], v_rs)
            t = qsc[:, 2 * D:3 * D]
            bm = qsc[:, 3 * D:3 * D + 64]
            rb = qsc[:, 3 * D + 64:3 * D + 128]
            e.tensor_reduce(out=bm, in_=t.rearrange("p (b k) -> p b k", k=16),
                            axis=AX.X, op=ALU.max, apply_absolute_value=True)
            e.drain()
            e.tensor_scalar(out=rb, in0=bm, scalar1=1e-30, scalar2=None, op0=ALU.max)
            e.drain()
            e.reciprocal(out=rb, in_=rb)
            e.drain()
            e.tensor_scalar(out=bm, in0=bm,
                            scalar1=smeta[:, OFF_RSTD1 + i:OFF_RSTD1 + i + 1],
                            scalar2=1.0 / 12.0, op0=ALU.mult, op1=ALU.mult)
            e.drain()
            e.tensor_tensor(out=qsc[:, D:2 * D].rearrange("p (b k) -> p b k", k=16),
                            in0=t.rearrange("p (b k) -> p b k", k=16),
                            in1=rb.unsqueeze(2).broadcast_to([128, 64, 16]),
                            op=ALU.mult).then_inc(h, 1)

        h_q2, v_q2 = sem.tick("act")

        @on("scalar")
        def _(e, i=i, v_q1=v_q1, h=h_q2):
            e.wait_ge(sem["vec"], v_q1)
            xn = qsc[:, D:2 * D]
            e.activation(out=xn, in_=xn, func=AF.Abs, bias=ZB()).then_inc(h, 1)

        h_q3, v_q3 = sem.tick("vec")

        @on("vector")
        def _(e, i=i, v_q2=v_q2, h=h_q3):
            e.wait_ge(sem["act"], v_q2)
            a = qsc[:, D:2 * D]
            q2v = qsc[:, 0:D]
            lvlb = qsc[:, 2 * D:3 * D]
            e._custom_dve(FP4_P1, out=q2v, in0=a, s0=2.5 / 6.0, s1=3.5 / 6.0,
                          imm2=5.0 / 6.0)
            e.drain()
            e._custom_dve(FP4_P2, out=lvlb, in0=a, in1=q2v,
                          s0=0.25 / 6.0, s1=0.75 / 6.0, imm2=1.25 / 6.0)
            e.drain()
            e.scalar_tensor_tensor(out=q2v, in0=a, scalar=1.75 / 6.0, in1=lvlb,
                                   op0=ALU.is_gt, op1=ALU.add).then_inc(h, 1)
        xq_vq3.append(v_q3)

        h_gx, v_gx = sem.tick("vec")

        @on("vector")
        def _(e, i=i, v_q3=v_q3, h=h_gx):
            if i >= 2:
                e.wait_ge(sem["txq"], 16 * (i - 1))
            ss = qsc[:, D:2 * D]
            bms = qsc[:, 3 * D:3 * D + 64]
            e.tensor_tensor(out=ss.rearrange("p (b k) -> p b k", k=16),
                            in0=qbf[:, 0:D].rearrange("p (b k) -> p b k", k=16),
                            in1=bms.unsqueeze(2).broadcast_to([128, 64, 16]),
                            op=ALU.mult)
            e.drain()
            e.tensor_tensor(out=xqt[:, i % 2, :], in0=qsc[:, 0:D], in1=ss,
                            op=ALU.mult).then_inc(h, 1)
        xq_gq.append(v_gx)

        h_tx, v_tx = sem.tick("txq", 16)

        @on("sync")
        def _(e, i=i, v_gx=v_gx, h=h_tx):
            e.wait_ge(sem["vec"], v_gx)
            if i >= 1:
                e.wait_ge(sem["txq"], 16 * i)
            if i >= 2:
                e.wait_ge(sem["xqs"], 16 * (i - 1))
            e.dma_start_transpose(xqs[:, i % 2, :, :],
                                  xqt[:, i % 2, :]).then_inc(h, 16)
        xq_tr.append(v_tx)

        h_xq2, v_xq2 = sem.tick("xqs", 16)

        @on("sync")
        def _(e, i=i, v_tx=v_tx, h=h_xq2):
            e.wait_ge(sem["txq"], v_tx)
            e.dma_start(out=xq_d.ap()[:, :, i * 128:(i + 1) * 128],
                        in_=xqs[:, i % 2, :, :]).then_inc(h, 16)
        xqs_st.append(v_xq2)
    dbg("x1", x1t[:, (NT - 1) % 2, :], [("vec", x1_done[-1])])
    mark("p5")

    P5_FENCE = [("mm", sem.val("mm")), ("vec", sem.val("vec")),
                ("act", sem.val("act")), ("gq", sem.val("gq")),
                ("tyg", sem.val("tyg")), ("txq", sem.val("txq")),
                ("xqs", sem.val("xqs")), ("tr", sem.val("tr")),
                ("stx0", sem.val("stx0")), ("stx1", sem.val("stx1")),
                ("url0", sem.val("url0")), ("url1", sem.val("url1")),
                ("gwld", WGT), ("wobld", WOB)]
    A.free("lnuH0")
    A.free("wgt")
    A.free("wob")
    A.free("gtok")
    A.free("ygT")
    A.free("urest")
    A.free("qsc")
    A.free("xqt")
    A.free("xqs")
    A.free("qbf")
    A.free("x1t")
    A.free("ascr")
    A.free("lnut")
    wgq = A.alloc("wgq", 8 * I_FFN * 2, BF16, [128, 8, I_FFN])
    wuq = A.alloc("wuq", 8 * I_FFN * 2, BF16, [128, 8, I_FFN])
    wdq = A.alloc("wdq", 22 * 1024 * 2, BF16, [128, 22, 1024])
    xqg = A.alloc("xqg", 2 * 8 * 128 * 2, BF16, [128, 2, 8, 128])
    hsil = A.alloc("hsil", 1024 * 4, F32, [128, 1024])
    hqh = A.alloc("hqh", 2 * 1024 * 4, F32, [128, 2, 1024])
    qff = A.alloc("qff", 2 * 1024 * 4, F32, [128, 2, 1024])
    sgnb = A.alloc("sgnb", 2 * 1024 * 2, BF16, [128, 2, 1024])
    hqs = A.alloc("hqs", 2 * 1024 * 2, BF16, [128, 2, 1024])
    hqT = A.alloc("hqT", 2 * 22 * 128 * 2, BF16, [128, 2, 22, 128])
    x1f = A.alloc("x1f", 2 * D * 4, F32, [128, 2, D])
    qm = A.alloc("qm", 2 * 128 * 4, F32, [128, 2, 128])

    for c in range(8):
        for qn, buf, base in (("sync", wgq, 0), ("scalar", wuq, WQ_SEC)):
            h, v = sem.tick("rbw", 16)

            @on(qn)
            def _(e, c=c, buf=buf, base=base, h=h, first=(c == 0)):
                e.wait_ge(sem["cc"], V_CCW)
                if first:
                    for s, vv in P5_FENCE:
                        e.wait_ge(sem[s], vv)
                src = ccw_out.ap()[c, base:base + WQ_SEC] \
                    .rearrange("(p a b) -> p a b", p=128, b=NSH)
                e.dma_start(out=buf[:, :, c * NSH:(c + 1) * NSH],
                            in_=src).then_inc(h, 16)
    RBW_ALL = sem.val("rbw")

    for c in range(8):
        h, v = sem.tick("rbd", 16)

        @on("gpsimd")
        def _(e, c=c, h=h, first=(c == 0)):
            e.wait_ge(sem["cc"], V_CCW)
            if first:
                for s, vv in P5_FENCE:
                    e.wait_ge(sem[s], vv)
            src = ccw_out.ap()[c, 2 * WQ_SEC:3 * WQ_SEC] \
                .rearrange("(p a b) -> p a b", p=128, b=128)
            e.dma_start(out=wdq[:, :, c * 128:(c + 1) * 128],
                        in_=src).then_inc(h, 16)
    RBD_ALL = sem.val("rbd")
    dbg("wgq", wgq[:, 0, 0:512], [("rbw", RBW_ALL)])
    mark("rb")

    SECS = ((0, 1024), (1024, 1024), (2048, 768))
    s3_last = {}
    s4_done = {}
    out_ev = {}
    thq_v = []
    gq_v = []
    th_of_tile = {}
    xgl_v = {}
    gsec = 0

    def emit_s4(tt):
        tpar = tt % 2
        b0 = 0 if tt % 2 == 0 else 4
        h_s4, v_s4 = sem.tick("mm")

        @on("tensor")
        def _(e, tt=tt, tpar=tpar, b0=b0, v_th=th_of_tile[tt], h=h_s4,
              _gE=take_bank(b0), _gF=take_bank(b0 + 1), first=(tt == 0)):
            _w(e, _gE)
            _w(e, _gF)
            e.wait_ge(sem["thq"], v_th)
            if first:
                e.wait_ge(sem["rbd"], RBD_ALL)
            last = None
            for kc in range(22):
                st = hqT[:, tpar, kc, :]
                last = e.matmul(ps[:, b0, :], st, wdq[:, kc, 0:512],
                                start=(kc == 0), stop=(kc == 21))
                last = e.matmul(ps[:, b0 + 1, :], st, wdq[:, kc, 512:1024],
                                start=(kc == 0), stop=(kc == 21))
            last.then_inc(h, 1)
        s4_done[tt] = v_s4

        xsem = f"xrl{tt % 2}"
        h_xr, v_xr = sem.tick(xsem, 16)

        @on("sync")
        def _(e, tt=tt, h=h_xr, xsem=xsem):
            if tt >= 2:
                e.wait_ge(sem["outs"], 16 * (tt - 1))
                e.wait_ge(sem[xsem], 16 * (tt // 2))
            else:
                for s, vv in P5_FENCE:
                    e.wait_ge(sem[s], vv)
            e.wait_ge(sem[f"stx{tt % 2}"], 16 * (NT // 2))
            e.dma_start(out=x1f[:, tt % 2, :],
                        in_=x1_d[tt * 128:(tt + 1) * 128, :]).then_inc(h, 16)

        h_oe, v_oe = sem.tick("vec")

        @on("vector")
        def _(e, tt=tt, b0=b0, v_s4=s4_done[tt], v_xr=v_xr, xsem=xsem, h=h_oe):
            e.wait_ge(sem["mm"], v_s4)
            e.wait_ge(sem[xsem], v_xr)
            if tt >= 2:
                e.wait_ge(sem["outs"], 16 * (tt - 1))
            src = ps[:, b0:b0 + 2, :].rearrange("p a b -> p (a b)")
            e.tensor_tensor(out=x1f[:, tt % 2, :], in0=src,
                            in1=x1f[:, tt % 2, :], op=ALU.add).then_inc(h, 1)
        out_ev[tt] = v_oe
        guard_bank(b0, "vec", v_oe)
        guard_bank(b0 + 1, "vec", v_oe)

        h_os, v_os = sem.tick("outs", 16)

        @on("sync")
        def _(e, tt=tt, v_oe=v_oe, h=h_os):
            e.wait_ge(sem["vec"], v_oe)
            if tt >= 1:
                e.wait_ge(sem["outs"], 16 * tt)
            e.dma_start(out=out_t.ap()[tt * 128:(tt + 1) * 128, :],
                        in_=x1f[:, tt % 2, :]).then_inc(h, 16)

    def emit_xgl(t):
        h_xl, v_xl = sem.tick("xgl", 16)

        @on("gpsimd")
        def _(e, t=t, h=h_xl):
            e.wait_ge(sem["xqs"], xqs_st[t])
            if t == 0:
                for s, vv in P5_FENCE:
                    e.wait_ge(sem[s], vv)
            if t >= 2:
                e.wait_ge(sem["mm"], s3_last[t - 2])
                e.wait_ge(sem["xgl"], 16 * (t - 1))
            e.dma_start(out=xqg[:, t % 2],
                        in_=xq_d.ap()[:, :, t * 128:(t + 1) * 128]).then_inc(h, 16)
        xgl_v[t] = v_xl

    emit_xgl(0)
    for tt in range(NT):
        tpar = tt % 2
        v_xl = xgl_v[tt]

        for si, (s0, Ls) in enumerate(SECS):
            sp = gsec % 2
            bg = 0 if sp == 0 else 4
            bu = 2 if sp == 0 else 6
            c1 = Ls - 512
            h_mm, v_mm = sem.tick("mm")

            @on("tensor")
            def _(e, tt=tt, tpar=tpar, s0=s0, Ls=Ls, c1=c1, bg=bg, bu=bu,
                  v_xl=v_xl, h=h_mm,
                  first=(tt == 0 and si == 0), fw=(si == 0),
                  _gA=take_bank(bg), _gB=take_bank(bg + 1),
                  _gC=take_bank(bu), _gD=take_bank(bu + 1)):
                for gg in (_gA, _gB, _gC, _gD):
                    _w(e, gg)
                if first:
                    e.wait_ge(sem["rbw"], RBW_ALL)
                if fw:
                    e.wait_ge(sem["xgl"], v_xl)
                last = None
                for k in range(8):
                    st = xqg[:, tpar, k, :]
                    last = e.matmul(ps[:, bg, :], st, wgq[:, k, s0:s0 + 512],
                                    start=(k == 0), stop=(k == 7))
                    last = e.matmul(ps[:, bg + 1, 0:c1], st,
                                    wgq[:, k, s0 + 512:s0 + Ls],
                                    start=(k == 0), stop=(k == 7))
                    last = e.matmul(ps[:, bu, :], st, wuq[:, k, s0:s0 + 512],
                                    start=(k == 0), stop=(k == 7))
                    last = e.matmul(ps[:, bu + 1, 0:c1], st,
                                    wuq[:, k, s0 + 512:s0 + Ls],
                                    start=(k == 0), stop=(k == 7))
                last.then_inc(h, 1)

            h_sg, v_sg = sem.tick("act")

            @on("scalar")
            def _(e, Ls=Ls, bg=bg, v_mm=v_mm, h=h_sg,
                  _gh=take_buf("hsil")):
                e.wait_ge(sem["mm"], v_mm)
                _w(e, _gh)
                gsrc = ps[:, bg:bg + 2, :].rearrange("p a b -> p (a b)")[:, 0:Ls]
                e.activation(out=hsil[:, 0:Ls], in_=gsrc,
                             func=AF.Silu, bias=ZB()).then_inc(h, 1)
            guard_bank(bg, "act", v_sg)
            guard_bank(bg + 1, "act", v_sg)

            h_hv, v_hv = sem.tick("vec")

            @on("vector")
            def _(e, Ls=Ls, sp=sp, bu=bu, v_sg=v_sg, h=h_hv,
                  vGp=(gq_v[gsec - 2] if gsec >= 2 else None)):
                e.wait_ge(sem["act"], v_sg)
                if vGp is not None:
                    e.wait_ge(sem["gq"], vGp)
                usrc = ps[:, bu:bu + 2, :].rearrange("p a b -> p (a b)")[:, 0:Ls]
                e.tensor_tensor(out=hqh[:, sp, 0:Ls], in0=usrc,
                                in1=hsil[:, 0:Ls], op=ALU.mult)
                e.drain()
                NB = Ls // 16
                hh = hqh[:, sp, 0:Ls]
                bm = qm[:, sp, 0:NB]
                rb = qm[:, sp, 64:64 + NB]
                e.tensor_reduce(out=bm, in_=hh.rearrange("p (b k) -> p b k", k=16),
                                axis=AX.X, op=ALU.max, apply_absolute_value=True)
                e.drain()
                e.tensor_scalar(out=rb, in0=bm, scalar1=1e-30, scalar2=None,
                                op0=ALU.max)
                e.drain()
                e.reciprocal(out=rb, in_=rb)
                e.drain()
                e.tensor_scalar(out=bm, in0=bm, scalar1=1.0 / 12.0,
                                scalar2=None, op0=ALU.mult).then_inc(h, 1)
            guard_bank(bu, "vec", v_hv)
            guard_bank(bu + 1, "vec", v_hv)
            guard_buf("hsil", "vec", v_hv)

            h_sn, v_sn = sem.tick("act")

            @on("scalar")
            def _(e, Ls=Ls, sp=sp, v_hv=v_hv, h=h_sn,
                  vGp=(gq_v[gsec - 2] if gsec >= 2 else None)):
                e.wait_ge(sem["vec"], v_hv)
                if vGp is not None:
                    e.wait_ge(sem["gq"], vGp)
                hh = hqh[:, sp, 0:Ls]
                e.activation(out=sgnb[:, sp, 0:Ls], in_=hh, func=AF.Sign,
                             bias=ZB())
                e.drain()
                e.activation(out=hh, in_=hh, func=AF.Abs,
                             bias=ZB()).then_inc(h, 1)

            h_q3, v_q3 = sem.tick("vec")

            @on("vector")
            def _(e, Ls=Ls, sp=sp, v_sn=v_sn, h=h_q3,
                  vth=(thq_v[gsec - 2] if gsec >= 2 else None)):
                e.wait_ge(sem["act"], v_sn)
                if vth is not None:
                    e.wait_ge(sem["thq"], vth)
                NB = Ls // 16
                a = hqh[:, sp, 0:Ls]
                rb = qm[:, sp, 64:64 + NB]
                e.tensor_tensor(out=a.rearrange("p (b k) -> p b k", k=16),
                                in0=a.rearrange("p (b k) -> p b k", k=16),
                                in1=rb.unsqueeze(2).broadcast_to([128, NB, 16]),
                                op=ALU.mult)
                e.drain()
                e._custom_dve(FP4_P1, out=qff[:, sp, 0:Ls], in0=a,
                              s0=2.5 / 6.0, s1=3.5 / 6.0, imm2=5.0 / 6.0)
                e.drain()
                e._custom_dve(FP4_P2, out=hqs[:, sp, 0:Ls], in0=a,
                              in1=qff[:, sp, 0:Ls],
                              s0=0.25 / 6.0, s1=0.75 / 6.0, imm2=1.25 / 6.0)
                e.drain()
                e.scalar_tensor_tensor(out=qff[:, sp, 0:Ls], in0=a,
                                       scalar=1.75 / 6.0, in1=hqs[:, sp, 0:Ls],
                                       op0=ALU.is_gt, op1=ALU.add).then_inc(h, 1)

            h_gq2, v_gq2 = sem.tick("gq")

            @on("gpsimd")
            def _(e, Ls=Ls, sp=sp, v_q3=v_q3, h=h_gq2,
                  vth=(thq_v[gsec - 2] if gsec >= 2 else None)):
                e.wait_ge(sem["vec"], v_q3)
                if vth is not None:
                    e.wait_ge(sem["thq"], vth)
                NB = Ls // 16
                bm = qm[:, sp, 0:NB]
                ss = hqh[:, sp, 0:Ls]
                e.tensor_tensor(out=ss.rearrange("p (b k) -> p b k", k=16),
                                in0=sgnb[:, sp, 0:Ls]
                                .rearrange("p (b k) -> p b k", k=16),
                                in1=bm.unsqueeze(2).broadcast_to([128, NB, 16]),
                                op=ALU.mult)
                e.drain()
                e.tensor_tensor(out=hqs[:, sp, 0:Ls], in0=qff[:, sp, 0:Ls],
                                in1=ss, op=ALU.mult).then_inc(h, 1)
            gq_v.append(v_gq2)
            if si == 0 and tt + 1 < NT:
                emit_xgl(tt + 1)

            h_th, v_th = sem.tick("thq", 16)

            @on("sync")
            def _(e, tt=tt, tpar=tpar, s0=s0, Ls=Ls, sp=sp, gsec=gsec,
                  v_gq2=v_gq2, h=h_th,
                  vs4=(s4_done.get(tt - 2) if tt >= 2 else None)):
                e.wait_ge(sem["gq"], v_gq2)
                if gsec >= 1:
                    e.wait_ge(sem["thq"], 16 * gsec)
                if vs4 is not None:
                    e.wait_ge(sem["mm"], vs4)
                e.dma_start_transpose(
                    hqT[:, tpar, s0 // 128:(s0 + Ls) // 128, :],
                    hqs[:, sp, 0:Ls]).then_inc(h, 16)
            thq_v.append(v_th)
            gsec += 1
        th_of_tile[tt] = thq_v[-1]
        s3_last[tt] = v_mm
        if tt >= 1:
            emit_s4(tt - 1)
    emit_s4(NT - 1)

    mark("all")
    if stage is not None:
        qlens, semsnap, ndbg = MARKS[stage]
        for k in Q:
            Q[k] = Q[k][:qlens[k]]
        dbg_list2 = dbg_list[:ndbg]
    else:
        semsnap = None
        dbg_list2 = dbg_list

    dbg_ticks = []
    for t, ap, waits in dbg_list2:
        h, v = sem.tick("outs", 16)
        dbg_ticks.append((t, ap, waits, h))
    OUTS_ALL = sem.val("outs")

    @blk.sync
    def _(e):
        for f in Q["sync"]:
            f(e)
        if semsnap is None:
            e.wait_ge(sem["outs"], OUTS_ALL)
        else:
            for k, v in semsnap.items():
                if v:
                    e.wait_ge(sem[k], v)
            e.wait_ge(sem["outs"], semsnap["outs"] + 16 * len(dbg_ticks))

    @blk.scalar
    def _(e):
        for f in Q["scalar"]:
            f(e)
        for t, ap, waits, h in dbg_ticks:
            for s, v in waits:
                e.wait_ge(sem[s], v)
            e.dma_start(out=t.ap(), in_=ap).then_inc(h, 16)

    @blk.vector
    def _(e):
        for f in Q["vector"]:
            f(e)

    @blk.tensor
    def _(e):
        for f in Q["tensor"]:
            f(e)

    @blk.gpsimd
    def _(e):
        for f in Q["gpsimd"]:
            f(e)

    es.close()
    mybir.codegen_inst_isa_subclasses(nc)
    return nc


def _in_maps(inputs):
    u = np.asarray(inputs["L0_attn_u"], np.float32)
    win = np.ascontiguousarray(np.asarray(inputs["L0_attn_W_in"], np.float32))
    wout = np.ascontiguousarray(np.asarray(inputs["L0_attn_W_out"], np.float32))
    wg = np.asarray(inputs["L0_ffn_w_gate"], np.float32)
    wu = np.asarray(inputs["L0_ffn_w_up"], np.float32)
    wd = np.asarray(inputs["L0_ffn_w_down"], np.float32)
    wB = np.asarray(inputs["L0_attn_wB"], np.float32)
    wC = np.asarray(inputs["L0_attn_wC"], np.float32)
    Bb = np.asarray(inputs["L0_attn_B_bias"], np.float32)
    Cb = np.asarray(inputs["L0_attn_C_bias"], np.float32)
    dtb = float(np.asarray(inputs["L0_attn_dt_bias"]).reshape(-1)[0])
    alog = float(np.asarray(inputs["L0_attn_A_log"]).reshape(-1)[0])
    vbc = np.zeros((128, 512), np.float32)
    vbc[:, 0:128] = wB[None, :]
    vbc[:, 128:256] = wC[None, :]
    vbc[:, 256:384] = Bb[None, :]
    vbc[:, 384:512] = Cb[None, :]
    ident = np.eye(128, dtype=np.float32)
    maps = []
    for c in range(8):
        b, h = c // 2, c % 2
        cstv = np.zeros((128, NCST), np.float32)
        cstv[:, C_EPS] = EPS
        cstv[64, C_AB] = dtb
        cstv[65, C_AB] = alog
        cstv[:, C_M0] = 1.0 if h == 0 else 0.0
        cstv[:, C_M1] = 1.0 if h == 1 else 0.0
        cstv[:, C_Q175] = 1.75
        cstv[:, C_ZERO] = 0.0
        cstv[:, C_ONE] = 1.0
        cstv[:, C_HPI] = math.pi / 2
        maps.append({
            "u": np.ascontiguousarray(u[b]),
            "uh": np.ascontiguousarray(u[b, h * LH:(h + 1) * LH]),
            "win": win,
            "wout": wout,
            "wg_sh": np.ascontiguousarray(wg[c * NSH:(c + 1) * NSH]),
            "wu_sh": np.ascontiguousarray(wu[c * NSH:(c + 1) * NSH]),
            "wd_sh": np.ascontiguousarray(wd[c * 128:(c + 1) * 128]),
            "vbc": vbc,
            "cst": cstv,
            "ident": ident,
        })
    return maps


def kernel(**inputs):
    nc = build_nc()
    maps = _in_maps(inputs)
    res = run_bass_kernel_spmd(nc, maps, core_ids=list(range(8)))
    out = np.empty((4, L, D), np.float32)
    for c, r in enumerate(res.results):
        b, h = c // 2, c % 2
        out[b, h * LH:(h + 1) * LH, :] = r["out"]
    return out
